# version 12
# baseline (speedup 1.0000x reference)
import copy as _copy
import numpy as np
from contextlib import ExitStack

import concourse.bass as bass
import concourse.mybir as mybir
from concourse.bass_utils import run_bass_kernel_spmd

F32 = mybir.dt.float32
BF16 = mybir.dt.bfloat16
AF = mybir.ActivationFunctionType
ALU = mybir.AluOpType

NCORES = 8
TOK = 2048
NTC = 16
TCG = 2
NG = NTC // TCG
D = 2048
KC = 16
H, HKV, DH = 16, 4, 128
FD, FCN = 768, 6
E, II, GUD = 8, 1408, 2816
ICN = 11
EPS_A = 1e-8
EPS_LN = 1e-5

W1_SLICES = [(0, 512), (512, 512), (1024, 384)]
MEMN_W = H * FCN * 129


class P:

    def __init__(self, nc, es):
        self.nc = nc
        names = ["lw", "lg", "pe", "ve", "se", "cc"]
        self.sem = {n: es.enter_context(nc.semaphore(n)) for n in names}
        self.cnt = {n: 0 for n in names}
        self.bank_free = {}

    def inc(self, instr, s, k=1):
        self.cnt[s] += k
        instr.then_inc(self.sem[s], k)
        return (s, self.cnt[s])

    def tok(self, s):
        return (s, self.cnt[s])

    def wait(self, eng, t):
        if t is not None and t[1] > 0:
            eng.wait_ge(self.sem[t[0]], t[1])

    def wait_all(self, eng, toks):
        for t in toks:
            self.wait(eng, t)

    def acquire(self, eng, bank):
        t = self.bank_free.get(bank)
        if t is not None:
            self.wait(eng, t)
            self.bank_free[bank] = None

    def release(self, bank, t):
        self.bank_free[bank] = t

    def full_barrier(self):
        engs = [self.nc.tensor, self.nc.vector, self.nc.scalar,
                self.nc.gpsimd, self.nc.sync]
        for e in engs:
            for s in ("lw", "lg", "pe", "ve", "se", "cc"):
                if self.cnt[s] > 0:
                    e.wait_ge(self.sem[s], self.cnt[s])

    def guard(self):
        nc = self.nc
        for s in ("pe", "se"):
            if self.cnt[s] > 0:
                nc.vector.wait_ge(self.sem[s], self.cnt[s])
        for s in ("pe", "ve"):
            if self.cnt[s] > 0:
                nc.scalar.wait_ge(self.sem[s], self.cnt[s])
        for s in ("ve", "se"):
            if self.cnt[s] > 0:
                nc.tensor.wait_ge(self.sem[s], self.cnt[s])

    def guard_syn(self):
        for s in ("pe", "ve", "se", "lg"):
            if self.cnt[s] > 0:
                self.nc.sync.wait_ge(self.sem[s], self.cnt[s])

    def dma(self, eng, s, out, in_, **kw):
        t = eng.dma_start(out=out, in_=in_, **kw)
        return self.inc(t, s, 16)


def build_core():
    nc = bass.Bass("TRN2", target_bir_lowering=False)

    dt = lambda n, s, k=None: (nc.dram_tensor(n, s, F32, kind=k) if k else
                               nc.dram_tensor(n, s, F32))
    x_d = nc.dram_tensor("x", [TOK, D], F32, kind="ExternalInput")
    wqT_d = nc.dram_tensor("WqT", [D, D], F32, kind="ExternalInput")
    wkT_d = nc.dram_tensor("WkT", [D, 512], F32, kind="ExternalInput")
    wvT_d = nc.dram_tensor("WvT", [D, 512], F32, kind="ExternalInput")
    wgT_d = nc.dram_tensor("WgT", [D, D], F32, kind="ExternalInput")
    wmbT_d = nc.dram_tensor("WmbT", [D, D], F32, kind="ExternalInput")
    gwT_d = nc.dram_tensor("gwT", [D, 8], F32, kind="ExternalInput")
    w1T_d = nc.dram_tensor("w1T", [E, D, GUD], F32, kind="ExternalInput")
    w2T_d = nc.dram_tensor("w2T", [E, II, D], F32, kind="ExternalInput")
    memn_d = nc.dram_tensor("memn", [128, MEMN_W], F32, kind="ExternalInput")
    bias_d = nc.dram_tensor("bpack", [1, 7168], F32, kind="ExternalInput")
    ident_d = nc.dram_tensor("ident", [128, 128], F32, kind="ExternalInput")
    out_d = nc.dram_tensor("out", [TOK, D], F32, kind="ExternalOutput")

    cache = {}
    for name, shape in [("wq", [D, D]), ("wk", [D, 512]), ("wv", [D, 512]),
                        ("wg", [D, D]), ("wmb", [D, D]),
                        ("w1", [E, D, GUD]), ("w2", [E, II, D])]:
        cache[name] = (nc.dram_tensor(f"c_{name}_h", shape, BF16),
                       nc.dram_tensor(f"c_{name}_l", shape, BF16))
    delta_d = nc.dram_tensor("delta", [128, MEMN_W], F32)
    ccout_d = nc.dram_tensor("ccout", [128, MEMN_W], F32)
    y1_d = nc.dram_tensor("y1", [TOK, D], F32)
    y1T_h_d = nc.dram_tensor("y1T_h", [NTC, 128, D], BF16)
    y1T_l_d = nc.dram_tensor("y1T_l", [NTC, 128, D], BF16)

    BOFF = {"q": 0, "k": 2048, "v": 2560, "g": 3072, "mb": 5120}

    es_perm = ExitStack()
    ident = es_perm.enter_context(nc.sbuf_tensor("ident_sb", [128, 128], F32))
    ones = es_perm.enter_context(nc.sbuf_tensor("ones_sb", [128, 128], F32))
    bias_sb = es_perm.enter_context(nc.sbuf_tensor("bias_sb", [1, 1024], F32))
    small = es_perm.enter_context(nc.sbuf_tensor("small_sb", [128, 448], F32))
    gwT_h = es_perm.enter_context(nc.sbuf_tensor("gwT_h", [128, 128], BF16))
    gwT_l = es_perm.enter_context(nc.sbuf_tensor("gwT_l", [128, 128], BF16))
    cmb_all = es_perm.enter_context(nc.sbuf_tensor("cmb_all", [128, 128], F32))
    wt1_h = es_perm.enter_context(nc.sbuf_tensor("wt1_h", [128, 8192], BF16))
    wt1_l = es_perm.enter_context(nc.sbuf_tensor("wt1_l", [128, 8192], BF16))
    wt2_h = es_perm.enter_context(nc.sbuf_tensor("wt2_h", [128, 8192], BF16))
    wt2_l = es_perm.enter_context(nc.sbuf_tensor("wt2_l", [128, 8192], BF16))

    C_NSQA, C_NSQB, C_S, C_RINV, C_MKSQ, C_RMKSQ = 0, 4, 8, 12, 16, 20
    C_DENP, C_COEF, C_RD = 24, 40, 56
    C_RDP = 320
    C_LN = 72
    C_GS = 88
    C_EPS = 96
    C_G8 = 104
    C_LGT = 160
    C_LTM = 288

    es_ps = ExitStack()
    psA = es_ps.enter_context(nc.psum_tensor("psA", [128, 512], F32))
    psB = es_ps.enter_context(nc.psum_tensor("psB", [128, 512], F32))
    psT1 = es_ps.enter_context(nc.psum_tensor("psT1", [128, 512], F32))
    psT2 = es_ps.enter_context(nc.psum_tensor("psT2", [128, 512], F32))
    psE1 = es_ps.enter_context(nc.psum_tensor("psE1", [128, 512], F32))
    psE2 = es_ps.enter_context(nc.psum_tensor("psE2", [128, 512], F32))
    psG = es_ps.enter_context(nc.psum_tensor("psG", [128, 512], F32))
    psX = es_ps.enter_context(nc.psum_tensor("psX", [128, 512], F32))

    es_sem = ExitStack()

    with nc.Block() as block:

        @block.sync
        def _(sync):
            p = P(nc, es_sem)
            TEN, VEC, ACT, GPS, SYN = nc.tensor, nc.vector, nc.scalar, nc.gpsimd, nc.sync

            def transpose_batch(srcs, dsts_h, dsts_l, src_tok, extra_wait=None):
                ev_tok = None
                for i0 in range(0, len(srcs), 4):
                    batch = srcs[i0:i0 + 4]
                    bank = psT1 if (i0 // 4) % 2 == 0 else psT2
                    bname = "psT1" if bank is psT1 else "psT2"
                    p.wait(TEN, src_tok)
                    if extra_wait is not None:
                        p.wait(TEN, extra_wait)
                    p.acquire(TEN, bname)
                    last = None
                    for j, s in enumerate(batch):
                        last = TEN.transpose(bank[:, j * 128:(j + 1) * 128], s, ident[:])
                    ptok = p.inc(last, "pe")
                    p.wait(VEC, ptok)
                    lastv = None
                    for j in range(len(batch)):
                        lastv = VEC.tensor_copy(dsts_h[i0 + j], bank[:, j * 128:(j + 1) * 128])
                    if dsts_l is not None:
                        for j in range(len(batch)):
                            lastv = VEC.tensor_sub(dsts_l[i0 + j], bank[:, j * 128:(j + 1) * 128], dsts_h[i0 + j])
                    ev_tok = p.inc(lastv, "ve")
                    p.release(bname, ev_tok)
                return ev_tok

            def load_wT_slice(hi_d, lo_d, k0, nkc, c0, w, dst_h, dst_l, dst_c0, stride):
                t = None
                for g in range(0, nkc, 4):
                    n = min(4, nkc - g)
                    src_h = hi_d.ap()[k0 + g * 128:k0 + (g + n) * 128, c0:c0 + w].rearrange(
                        "(kc p) w -> p kc w", p=128)
                    src_l = lo_d.ap()[k0 + g * 128:k0 + (g + n) * 128, c0:c0 + w].rearrange(
                        "(kc p) w -> p kc w", p=128)
                    dh = dst_h.ap()[:, dst_c0 + g * stride: dst_c0 + (g + n) * stride].rearrange(
                        "p (kc w) -> p kc w", kc=n)
                    dl = dst_l.ap()[:, dst_c0 + g * stride: dst_c0 + (g + n) * stride].rearrange(
                        "p (kc w) -> p kc w", kc=n)
                    p.dma(SYN, "lw", dh, src_h)
                    t = p.dma(SYN, "lw", dl, src_l)
                return t

            def mm3(bank, bank_name, xh_tiles, xl_tiles, wh_ap_fn, wl_ap_fn, w, nkc,
                    seed=None, waits=()):
                p.acquire(TEN, bank_name)
                for t in waits:
                    p.wait(TEN, t)
                first = True
                last = None
                if seed is not None:
                    last = TEN.matmul(bank[:, 0:w], ones[0:1, 0:128], seed,
                                      start=True, stop=False)
                    first = False
                for kc in range(nkc):
                    l = kc == nkc - 1
                    last = TEN.matmul(bank[:, 0:w], xh_tiles(kc), wh_ap_fn(kc),
                                      start=first, stop=False)
                    first = False
                    last = TEN.matmul(bank[:, 0:w], xh_tiles(kc), wl_ap_fn(kc),
                                      start=False, stop=False)
                    last = TEN.matmul(bank[:, 0:w], xl_tiles(kc), wh_ap_fn(kc),
                                      start=False, stop=l)
                return p.inc(last, "pe")

            def noldw(m):
                inst = _copy.replace(m.ins, ldweights=False)
                nc.register_instruction(inst, overwrite=True)

            def mm3m(xh_fn, xl_fn, movings, nkc, waits=()):
                for (bank, bn, _, _, _, _) in movings:
                    p.acquire(TEN, bn)
                for t in waits:
                    p.wait(TEN, t)
                started = [False] * len(movings)
                last = None
                for i, (bank, bn, mvh, mvl, w, seed) in enumerate(movings):
                    if seed is not None:
                        last = TEN.matmul(bank[:, 0:w], ones[0:1, 0:128], seed,
                                          start=True, stop=False)
                        started[i] = True
                pend = []
                for kc in range(nkc):
                    l = kc == nkc - 1
                    first_in_run = True
                    for i, (bank, bn, mvh, mvl, w, seed) in enumerate(movings):
                        last = TEN.matmul(bank[:, 0:w], xh_fn(kc), mvh(kc),
                                          start=not started[i], stop=False)
                        started[i] = True
                        if not first_in_run:
                            pend.append(last)
                        first_in_run = False
                        last = TEN.matmul(bank[:, 0:w], xh_fn(kc), mvl(kc),
                                          start=False, stop=False)
                        pend.append(last)
                    first_in_run = True
                    for i, (bank, bn, mvh, mvl, w, seed) in enumerate(movings):
                        last = TEN.matmul(bank[:, 0:w], xl_fn(kc), mvh(kc),
                                          start=False, stop=l)
                        if not first_in_run:
                            pend.append(last)
                        first_in_run = False
                tok = p.inc(last, "pe")
                for m in pend:
                    noldw(m)
                return tok

            def proj_pairs(slices, xs_tok, xh_of, xl_of, prev_used=None, tis=None):
                W = 256
                SL = KC * W
                if tis is None:
                    tis = list(range(TCG))
                sets = [(wt1_h, wt1_l, 0), (wt2_h, wt2_l, 512)]
                ngrp = (len(slices) + 1) // 2
                load_toks = {}
                mm_toks = {}

                def emit_load(gi):
                    hi_b, lo_b, bcol = sets[gi % 2]
                    if gi >= 2 and (gi - 2) in mm_toks:
                        p.wait(SYN, mm_toks[gi - 2])
                    elif gi < 2 and prev_used is not None:
                        p.wait(SYN, prev_used)
                    toks = []
                    for j, (hi_d, lo_d, c0, boff, dstf, act) in enumerate(slices[gi * 2: gi * 2 + 2]):
                        lt = load_wT_slice(hi_d, lo_d, 0, KC, c0, W, hi_b, lo_b, j * SL, W)
                        bt = p.dma(SYN, "lw", bias_sb[0:1, bcol + j * W: bcol + (j + 1) * W],
                                   bias_d.ap()[0:1, boff:boff + W])
                        toks += [lt, bt]
                    load_toks[gi] = toks

                emit_load(0)
                for gi in range(ngrp):
                    if gi + 1 < ngrp:
                        emit_load(gi + 1)
                    hi_b, lo_b, bcol = sets[gi % 2]
                    grp = slices[gi * 2: gi * 2 + 2]
                    for ti in tis:
                        bks = [(psA, "psA"), (psB, "psB")]
                        movings = []
                        for j in range(len(grp)):
                            movings.append((bks[j][0], bks[j][1],
                                            (lambda kc, j=j: hi_b[:, j * SL + kc * W: j * SL + (kc + 1) * W]),
                                            (lambda kc, j=j: lo_b[:, j * SL + kc * W: j * SL + (kc + 1) * W]),
                                            W, bias_sb[0:1, bcol + j * W: bcol + (j + 1) * W]))
                        mt = mm3m((lambda kc, ti=ti: xh_of(ti, kc)),
                                  (lambda kc, ti=ti: xl_of(ti, kc)),
                                  movings, KC, waits=[xs_tok] + load_toks[gi])
                        mm_toks[gi] = mt
                        for j, (hi_d, lo_d, c0, boff, dstf, act) in enumerate(grp):
                            bank, bn = bks[j]
                            dsl = dstf(ti)
                            if act is None:
                                p.wait(VEC, mt)
                                ev = VEC.tensor_copy(dsl, bank[:, 0:W])
                                et = p.inc(ev, "ve")
                            else:
                                p.wait(ACT, mt)
                                ev = ACT.activation(out=dsl, in_=bank[:, 0:W], func=act)
                                et = p.inc(ev, "se")
                            p.release(bn, et)
                return mm_toks[ngrp - 1]

            def ln_chain(z_ap, out_ap):
                c = C_LN
                zv = p.tok("ve")
                p.wait(ACT, zv)
                VEC.drain()
                t1 = VEC.reduce_sum(small[:, c:c + 1], z_ap, axis=mybir.AxisListType.X)
                vt = p.inc(t1, "ve")
                p.acquire(ACT, "psX")
                lasts = None
                for i in range(4):
                    lasts = ACT.activation(out=psX[:, 0:512], in_=z_ap[:, i * 512:(i + 1) * 512],
                                           func=AF.Square, accum_out=small[:, c + 1 + i:c + 2 + i])
                st = p.inc(lasts, "se")
                p.release("psX", st)
                p.wait(VEC, st)
                v = VEC.reduce_sum(small[:, c + 5:c + 6], small[:, c + 1:c + 5], axis=mybir.AxisListType.X)
                VEC.drain()
                VEC.tensor_scalar_mul(small[:, c + 6:c + 7], small[:, c:c + 1], 1.0 / D)
                VEC.tensor_scalar_mul(small[:, c + 7:c + 8], small[:, c + 5:c + 6], 1.0 / D)
                VEC.drain()
                v = VEC.tensor_mul(small[:, c + 8:c + 9], small[:, c + 6:c + 7], small[:, c + 6:c + 7])
                VEC.drain()
                v = VEC.tensor_sub(small[:, c + 8:c + 9], small[:, c + 7:c + 8], small[:, c + 8:c + 9])
                vt = p.inc(v, "ve")
                p.wait(ACT, vt)
                s = ACT.activation(out=small[:, c + 9:c + 10], in_=small[:, c + 8:c + 9],
                                   func=AF.Sqrt, bias=small[:, C_EPS:C_EPS + 1])
                st = p.inc(s, "se")
                p.wait(VEC, st)
                v = VEC.reciprocal(small[:, c + 9:c + 10], small[:, c + 9:c + 10])
                VEC.drain()
                v = VEC.scalar_tensor_tensor(out=small[:, c + 10:c + 11], in0=small[:, c + 6:c + 7],
                                             scalar=-1.0, in1=small[:, c + 9:c + 10],
                                             op0=ALU.mult, op1=ALU.mult)
                vt = p.inc(v, "ve")
                p.wait(ACT, vt)
                s = ACT.activation(out=out_ap, in_=z_ap, func=AF.Identity,
                                   scale=small[:, c + 9:c + 10], bias=small[:, c + 10:c + 11])
                return p.inc(s, "se")

            def dpfp(a_ap, mkr_ap, nheads, k_src_ap, pe_wait=None, ve_wait=None):
                if pe_wait is not None:
                    p.wait(VEC, pe_wait)
                    p.wait(ACT, pe_wait)
                if ve_wait is not None:
                    p.wait(ACT, ve_wait)
                lasts = None
                for j in range(nheads):
                    lasts = ACT.activation(out=a_ap[:, j * 256:j * 256 + 128],
                                           in_=k_src_ap[:, j * 128:(j + 1) * 128], func=AF.Relu)
                    lasts = ACT.activation(out=a_ap[:, j * 256 + 128:j * 256 + 256],
                                           in_=k_src_ap[:, j * 128:(j + 1) * 128], func=AF.Relu,
                                           scale=-1.0)
                at = p.inc(lasts, "se")
                p.wait(VEC, at)
                lastv = None
                for j in range(nheads):
                    ab = a_ap[:, j * 256:(j + 1) * 256]
                    ob = mkr_ap[:, j * 768:(j + 1) * 768]
                    for r in (1, 2, 3):
                        o0 = (r - 1) * 256
                        lastv = VEC.tensor_mul(ob[:, o0:o0 + r], ab[:, 0:r], ab[:, 256 - r:256])
                        lastv = VEC.tensor_mul(ob[:, o0 + r:o0 + 256], ab[:, r:256], ab[:, 0:256 - r])
                return p.inc(lastv, "ve")

            def mk_normalize(mkr_ap, nheads, c_nsq_base):
                p.wait(ACT, p.tok("ve"))
                p.acquire(ACT, "psX")
                lasts = None
                for j in range(nheads):
                    b = mkr_ap[:, j * 768:(j + 1) * 768]
                    lasts = ACT.activation(out=psX[:, 0:512], in_=b[:, 0:512], func=AF.Square,
                                           accum_out=small[:, C_NSQA + j:C_NSQA + j + 1])
                    lasts = ACT.activation(out=psX[:, 0:256], in_=b[:, 512:768], func=AF.Square,
                                           accum_out=small[:, C_NSQB + j:C_NSQB + j + 1])
                st = p.inc(lasts, "se")
                p.release("psX", st)
                p.wait(VEC, st)
                v = VEC.tensor_add(small[:, C_NSQA:C_NSQA + nheads],
                                   small[:, C_NSQA:C_NSQA + nheads],
                                   small[:, C_NSQB:C_NSQB + nheads])
                vt = p.inc(v, "ve")
                p.wait(ACT, vt)
                s = ACT.activation(out=small[:, C_S:C_S + nheads], in_=small[:, C_NSQA:C_NSQA + nheads],
                                   func=AF.Sqrt)
                st = p.inc(s, "se")
                p.wait(VEC, st)
                VEC.tensor_scalar_max(small[:, C_S:C_S + nheads], small[:, C_S:C_S + nheads], 1e-12)
                VEC.drain()
                VEC.reciprocal(small[:, C_RINV:C_RINV + nheads], small[:, C_S:C_S + nheads])
                VEC.drain()
                VEC.tensor_mul(small[:, C_MKSQ:C_MKSQ + nheads], small[:, C_NSQA:C_NSQA + nheads],
                               small[:, C_RINV:C_RINV + nheads])
                VEC.drain()
                VEC.tensor_mul(small[:, C_MKSQ:C_MKSQ + nheads], small[:, C_MKSQ:C_MKSQ + nheads],
                               small[:, C_RINV:C_RINV + nheads])
                VEC.drain()
                VEC.tensor_scalar_max(small[:, C_RMKSQ:C_RMKSQ + nheads],
                                      small[:, C_MKSQ:C_MKSQ + nheads], 1e-30)
                VEC.drain()
                v = VEC.reciprocal(small[:, C_RMKSQ:C_RMKSQ + nheads],
                                   small[:, C_RMKSQ:C_RMKSQ + nheads])
                return p.inc(v, "ve")

            t = SYN.dma_start(out=ident[:], in_=ident_d.ap())
            tok_id = p.inc(t, "lw", 16)
            VEC.memset(ones[:], 1.0)
            v = VEC.memset(small[:, C_EPS:C_EPS + 1], EPS_LN)
            tok_ones = p.inc(v, "ve")

            with nc.sbuf_tensor("memn_sb", [128, MEMN_W], F32) as memn, \
                 nc.sbuf_tensor("x_tm", [128, D], F32) as x_tm, \
                 nc.sbuf_tensor("xT_h", [128, TCG * D], BF16) as xT_h, \
                 nc.sbuf_tensor("xT_l", [128, TCG * D], BF16) as xT_l:

                t = SYN.dma_start(out=memn[:], in_=memn_d.ap())
                tok_memn = p.inc(t, "lw", 16)

                with nc.sbuf_tensor("gwf", [128, 128], F32) as gwf:
                    for kc in range(KC):
                        t = p.dma(SYN, "lw", gwf[:, kc * 8:(kc + 1) * 8],
                                  gwT_d.ap()[kc * 128:(kc + 1) * 128, :])
                    p.wait(VEC, t)
                    VEC.tensor_copy(gwT_h[:], gwf[:])
                    VEC.drain()
                    v = VEC.tensor_sub(gwT_l[:], gwf[:], gwT_h[:])
                    p.inc(v, "ve")

                    v = VEC.memset(gwf[:], 0.0)
                    zt = p.inc(v, "ve")
                    p.wait(GPS, zt)
                    for i in range(0, MEMN_W, 128):
                        w = min(128, MEMN_W - i)
                        tok_dz = p.dma(GPS, "lg", delta_d.ap()[:, i:i + w], gwf[:, 0:w])
                    p.full_barrier()

                with nc.sbuf_tensor("stg0", [128, GUD], F32) as stg0, \
                     nc.sbuf_tensor("stg1", [128, GUD], F32) as stg1, \
                     nc.sbuf_tensor("sph0", [128, GUD], BF16) as sph0, \
                     nc.sbuf_tensor("sph1", [128, GUD], BF16) as sph1, \
                     nc.sbuf_tensor("spl0", [128, GUD], BF16) as spl0, \
                     nc.sbuf_tensor("spl1", [128, GUD], BF16) as spl1:
                    stg = [stg0, stg1]
                    sph = [sph0, sph1]
                    spl = [spl0, spl1]
                    band_split_tok = {}
                    band_store_tok = {}
                    bands = []
                    for nm, src, ktot, dout in [
                        ("wk", wkT_d, D, 512), ("wv", wvT_d, D, 512),
                        ("wmb", wmbT_d, D, D), ("wq", wqT_d, D, D),
                        ("wg", wgT_d, D, D)]:
                        for b in range(ktot // 128):
                            bands.append((nm, src.ap()[b * 128:(b + 1) * 128, :],
                                          cache[nm][0].ap()[b * 128:(b + 1) * 128, :],
                                          cache[nm][1].ap()[b * 128:(b + 1) * 128, :], dout))
                    for e in range(E):
                        for b in range(KC):
                            bands.append(("w1", w1T_d.ap()[e, b * 128:(b + 1) * 128, :],
                                          cache["w1"][0].ap()[e, b * 128:(b + 1) * 128, :],
                                          cache["w1"][1].ap()[e, b * 128:(b + 1) * 128, :], GUD))
                    for e in range(E):
                        for b in range(ICN):
                            bands.append(("w2", w2T_d.ap()[e, b * 128:(b + 1) * 128, :],
                                          cache["w2"][0].ap()[e, b * 128:(b + 1) * 128, :],
                                          cache["w2"][1].ap()[e, b * 128:(b + 1) * 128, :], D))

                    for bi, (nm, src, dsth, dstl, dout) in enumerate(bands):
                        s = bi % 2
                        if bi >= 2:
                            p.wait(SYN, band_split_tok[bi - 2])
                        t = SYN.dma_start(out=stg[s][:, 0:dout], in_=src)
                        lt = p.inc(t, "lw", 16)
                        p.wait(ACT, lt)
                        if bi >= 2:
                            p.wait(ACT, band_store_tok[bi - 2])
                        sc = ACT.activation(out=sph[s][:, 0:dout], in_=stg[s][:, 0:dout],
                                            func=AF.Copy)
                        ht = p.inc(sc, "se")
                        p.wait(VEC, ht)
                        if bi >= 2:
                            p.wait(VEC, band_store_tok[bi - 2])
                        v = VEC.tensor_sub(spl[s][:, 0:dout], stg[s][:, 0:dout], sph[s][:, 0:dout])
                        st = p.inc(v, "ve")
                        band_split_tok[bi] = st
                        p.wait(GPS, st)
                        p.dma(GPS, "lg", dsth, sph[s][:, 0:dout])
                        band_store_tok[bi] = p.dma(GPS, "lg", dstl, spl[s][:, 0:dout])

                    p.full_barrier()

                with nc.sbuf_tensor("kraw", [128, 2 * 512], F32) as kraw, \
                     nc.sbuf_tensor("v_sb", [128, 2 * 512], F32) as v_sb, \
                     nc.sbuf_tensor("mb_sb", [128, 2 * D], F32) as mb_sb, \
                     nc.sbuf_tensor("a_sb", [128, 1024], F32) as a_sb, \
                     nc.sbuf_tensor("mk_sb", [128, HKV * 768], F32) as mk_sb, \
                     nc.sbuf_tensor("mkT_sb", [128, HKV * 768], F32) as mkT_sb, \
                     nc.sbuf_tensor("rhsg", [128, H * 129], F32) as rhsg, \
                     nc.sbuf_tensor("dtmp", [128, 258], F32) as dtmp, \
                     nc.sbuf_tensor("ttmp", [128, 512], F32) as ttmp:

                    ttmp2 = a_sb

                    dtmp_tok = [None, None]
                    cend_pe = {}
                    cdpfp = {}
                    xT_read = {}
                    xtm_read = None
                    proj_done = {}
                    proj_mm = {}
                    kraw_read = {}

                    def p1_T(g):
                        nonlocal xtm_read
                        sl = g % 2
                        if xtm_read is not None:
                            p.wait(SYN, xtm_read)
                        t = SYN.dma_start(out=x_tm[:], in_=x_d.ap()[g * 128:(g + 1) * 128, :])
                        xt = p.inc(t, "lw", 16)
                        srcs = [x_tm[:, kc * 128:(kc + 1) * 128] for kc in range(KC)]
                        dsth = [xT_h[:, (sl * KC + kc) * 128:(sl * KC + kc + 1) * 128] for kc in range(KC)]
                        dstl = [xT_l[:, (sl * KC + kc) * 128:(sl * KC + kc + 1) * 128] for kc in range(KC)]
                        et = transpose_batch(srcs, dsth, dstl, xt, extra_wait=xT_read.get(sl))
                        xtm_read = p.tok("pe")
                        return et

                    def p1_P(g, xs_tok):
                        sl = g % 2
                        slices = []
                        for ds in range(2):
                            slices.append((cache["wk"][0], cache["wk"][1], ds * 256, BOFF["k"] + ds * 256,
                                           (lambda ti, ds=ds: kraw[:, ti * 512 + ds * 256: ti * 512 + ds * 256 + 256]), None))
                        for ds in range(2):
                            slices.append((cache["wv"][0], cache["wv"][1], ds * 256, BOFF["v"] + ds * 256,
                                           (lambda ti, ds=ds: v_sb[:, ti * 512 + ds * 256: ti * 512 + ds * 256 + 256]), None))
                        for ds in range(8):
                            slices.append((cache["wmb"][0], cache["wmb"][1], ds * 256, BOFF["mb"] + ds * 256,
                                           (lambda ti, ds=ds: mb_sb[:, ti * D + ds * 256: ti * D + ds * 256 + 256]),
                                           AF.Sigmoid))
                        kt = kraw_read.get(g - 2)
                        if kt is not None:
                            for t_ in kt:
                                p.wait(VEC, t_)
                                p.wait(ACT, t_)
                        mt = proj_pairs(slices, xs_tok,
                                        (lambda ti, kc: xT_h[:, (ti * KC + kc) * 128:(ti * KC + kc + 1) * 128]),
                                        (lambda ti, kc: xT_l[:, (ti * KC + kc) * 128:(ti * KC + kc + 1) * 128]),
                                        prev_used=proj_mm.get(g - 1), tis=[sl])
                        proj_mm[g] = mt
                        xT_read[sl] = mt
                        proj_done[g] = [p.tok("ve"), p.tok("se")]

                    def p1_C(g):
                        ti = g % 2
                        p.wait_all(ACT, proj_done[g])
                        p.wait_all(VEC, proj_done[g])
                        mkt = dpfp(a_sb[:], mk_sb[:], HKV, kraw[:, ti * 512:(ti + 1) * 512],
                                   pe_wait=cend_pe.get(g - 1),
                                   ve_wait=(kraw_read.get(g - 1) or [None, None])[1])
                        cdpfp[g] = mkt
                        nt = mk_normalize(mk_sb[:], HKV, C_NSQA)
                        srcs = []
                        dsth = []
                        for j in range(HKV):
                            for fc in range(FCN):
                                srcs.append(mk_sb[:, j * 768 + fc * 128: j * 768 + (fc + 1) * 128])
                                dsth.append(mkT_sb[:, (j * FCN + fc) * 128:(j * FCN + fc + 1) * 128])
                        mkT_tok = transpose_batch(srcs, dsth, None, nt)

                        for kv in range(HKV):
                            tiles = [(psE1, "psE1", i * 130) for i in range(3)] + [(psE2, "psE2", 0)]
                            p.acquire(TEN, "psE1")
                            p.acquire(TEN, "psE2")
                            p.wait(TEN, mkT_tok)
                            p.wait(TEN, tok_memn)
                            last = None
                            for fc in range(FCN):
                                for j in range(4):
                                    h = kv * 4 + j
                                    bank, bn, off = tiles[j]
                                    mm = TEN.matmul(bank[:, off:off + 129],
                                                    mkT_sb[:, (kv * FCN + fc) * 128:(kv * FCN + fc + 1) * 128],
                                                    memn[:, (h * FCN + fc) * 129:(h * FCN + fc + 1) * 129],
                                                    start=(fc == 0 and j in (0, 3)),
                                                    stop=(fc == FCN - 1), skip_group_check=True)
                                    if j > 0:
                                        noldw(mm)
                                    last = mm
                            mt = p.inc(last, "pe")
                            h0 = kv * 4
                            p.wait(VEC, mt)
                            for j in range(3):
                                VEC.tensor_copy(small[:, C_DENP + h0 + j:C_DENP + h0 + j + 1],
                                                psE1[:, j * 130 + 128:j * 130 + 129])
                            VEC.tensor_copy(small[:, C_DENP + h0 + 3:C_DENP + h0 + 4], psE2[:, 128:129])
                            VEC.drain()
                            VEC.tensor_scalar(small[:, C_DENP + h0:C_DENP + h0 + 4],
                                              small[:, C_DENP + h0:C_DENP + h0 + 4],
                                              small[:, C_RINV + kv:C_RINV + kv + 1], EPS_A,
                                              op0=ALU.mult, op1=ALU.add)
                            VEC.drain()
                            VEC.reciprocal(small[:, C_RD + h0:C_RD + h0 + 4],
                                           small[:, C_DENP + h0:C_DENP + h0 + 4])
                            VEC.drain()
                            v = VEC.tensor_scalar_mul(small[:, C_RDP + h0:C_RDP + h0 + 4],
                                                      small[:, C_RD + h0:C_RD + h0 + 4],
                                                      small[:, C_RINV + kv:C_RINV + kv + 1])
                            vt = p.inc(v, "ve")
                            p.wait(ACT, vt)
                            lasts = None
                            for j in range(4):
                                h = h0 + j
                                bank, bn, off = tiles[j]
                                lasts = ACT.activation(out=ttmp[:, j * 128:(j + 1) * 128],
                                                       in_=bank[:, off:off + 128], func=AF.Identity,
                                                       scale=small[:, C_RDP + h:C_RDP + h + 1])
                            st = p.inc(lasts, "se")
                            p.release("psE1", st)
                            p.release("psE2", st)
                            p.wait(VEC, st)
                            lastv = None
                            for j in range(4):
                                h = h0 + j
                                lastv = VEC.tensor_sub(
                                    ttmp2[:, j * 128:(j + 1) * 128],
                                    v_sb[:, ti * 512 + kv * 128: ti * 512 + (kv + 1) * 128],
                                    ttmp[:, j * 128:(j + 1) * 128])
                            for j in range(4):
                                h = h0 + j
                                lastv = VEC.scalar_tensor_tensor(
                                    out=rhsg[:, h * 129:h * 129 + 128], in0=ttmp2[:, j * 128:(j + 1) * 128],
                                    scalar=small[:, C_RINV + kv:C_RINV + kv + 1],
                                    in1=mb_sb[:, ti * D + h * 128: ti * D + (h + 1) * 128],
                                    op0=ALU.mult, op1=ALU.mult)
                            p.inc(lastv, "ve")

                        for kv in range(HKV):
                            VEC.tensor_scalar_mul(small[:, C_COEF + kv * 4:C_COEF + (kv + 1) * 4],
                                                  small[:, C_DENP + kv * 4:C_DENP + (kv + 1) * 4],
                                                  small[:, C_RMKSQ + kv:C_RMKSQ + kv + 1])
                        VEC.drain()
                        VEC.tensor_scalar(small[:, C_COEF:C_COEF + H], small[:, C_COEF:C_COEF + H],
                                          -1.0, 1.0, op0=ALU.mult, op1=ALU.add)
                        VEC.drain()
                        VEC.tensor_scalar_max(small[:, C_COEF:C_COEF + H], small[:, C_COEF:C_COEF + H], 0.0)
                        VEC.drain()
                        VEC.tensor_scalar_min(small[:, C_COEF:C_COEF + H], small[:, C_COEF:C_COEF + H], 1.0)
                        VEC.drain()
                        for kv in range(HKV):
                            VEC.tensor_scalar_mul(small[:, C_COEF + kv * 4:C_COEF + (kv + 1) * 4],
                                                  small[:, C_COEF + kv * 4:C_COEF + (kv + 1) * 4],
                                                  small[:, C_RINV + kv:C_RINV + kv + 1])
                        VEC.drain()
                        v = VEC.tensor_copy(rhsg.ap().rearrange("p (h w) -> p h w", w=129)[:, :, 128],
                                            small[:, C_COEF:C_COEF + H])
                        rhs_tok = p.inc(v, "ve")

                        for h in range(H):
                            kv = h // 4
                            for fc in range(FCN):
                                i = h * FCN + fc
                                bank, bn = (psE1, "psE1") if i % 2 == 0 else (psE2, "psE2")
                                p.acquire(TEN, bn)
                                p.wait(TEN, rhs_tok)
                                last = TEN.matmul(bank[:, 0:129],
                                                  mk_sb[:, kv * 768 + fc * 128: kv * 768 + (fc + 1) * 128],
                                                  rhsg[:, h * 129:(h + 1) * 129],
                                                  start=True, stop=True)
                                mt = p.inc(last, "pe")
                                slx = i % 2
                                if dtmp_tok[slx] is not None:
                                    p.wait(VEC, dtmp_tok[slx])
                                p.wait(VEC, mt)
                                ev = VEC.tensor_copy(dtmp[:, slx * 129:(slx + 1) * 129], bank[:, 0:129])
                                et = p.inc(ev, "ve")
                                p.release(bn, et)
                                p.wait(GPS, et)
                                p.wait(GPS, tok_dz)
                                d = GPS.dma_start(out=delta_d.ap()[:, i * 129:(i + 1) * 129],
                                                  in_=dtmp[:, slx * 129:(slx + 1) * 129],
                                                  accum_op=ALU.add)
                                dtmp_tok[slx] = p.inc(d, "lg", 16)
                        cend_pe[g] = p.tok("pe")

                    for g in range(NTC):
                        xs = p1_T(g)
                        p1_P(g, xs)
                        if g > 0:
                            p1_C(g - 1)
                    p1_C(NTC - 1)

                    p.full_barrier()
                    c = GPS.collective_compute(
                        "AllReduce", ALU.add,
                        replica_groups=[list(range(NCORES))],
                        ins=[delta_d.ap().opt()],
                        outs=[ccout_d.ap().opt()],
                    )
                    ct = p.inc(c, "cc")
                    p.wait(SYN, ct)
                    CH = 3072
                    for i in range((MEMN_W + CH - 1) // CH):
                        w = min(CH, MEMN_W - i * CH)
                        t = SYN.dma_start(out=mk_sb[:, 0:w] if i % 2 == 0 else mkT_sb[:, 0:w],
                                          in_=ccout_d.ap()[:, i * CH:i * CH + w])
                        lt = p.inc(t, "lw", 16)
                        p.wait(VEC, lt)
                        buf = mk_sb if i % 2 == 0 else mkT_sb
                        v = VEC.tensor_add(memn[:, i * CH:i * CH + w], memn[:, i * CH:i * CH + w],
                                           buf[:, 0:w])
                        vt = p.inc(v, "ve")
                        p.wait(SYN, vt)
                    p.full_barrier()

                with nc.sbuf_tensor("q_sb", [128, 2 * D], F32) as q_sb, \
                     nc.sbuf_tensor("gate_sb", [128, 2 * D], F32) as gate_sb, \
                     nc.sbuf_tensor("aq_sb", [128, 512], F32) as aq_sb, \
                     nc.sbuf_tensor("mq_sb", [128, 2 * 768], F32) as mq_sb, \
                     nc.sbuf_tensor("mqT_sb", [128, 2 * 768], F32) as mqT_sb, \
                     nc.sbuf_tensor("y1T_th", [128, D], BF16) as y1T_th, \
                     nc.sbuf_tensor("y1T_tl", [128, D], BF16) as y1T_tl, \
                     nc.sbuf_tensor("ttmp3", [128, 256], F32) as ttmp:

                    ttmp3b = aq_sb
                    xT_read = {}
                    xtm_read = None
                    xtm_free = None
                    proj_done = {}
                    proj_mm = {}
                    qread = {}
                    cend_pe = {}
                    cdpfp = {}
                    y1T_used = None
                    y1T_store_tok = None

                    def p3_T(g):
                        nonlocal xtm_read
                        sl = g % 2
                        if xtm_read is not None:
                            p.wait(SYN, xtm_read)
                        if xtm_free is not None:
                            for t_ in xtm_free:
                                p.wait(SYN, t_)
                        t = SYN.dma_start(out=x_tm[:], in_=x_d.ap()[g * 128:(g + 1) * 128, :])
                        xt = p.inc(t, "lw", 16)
                        srcs = [x_tm[:, kc * 128:(kc + 1) * 128] for kc in range(KC)]
                        dsth = [xT_h[:, (sl * KC + kc) * 128:(sl * KC + kc + 1) * 128] for kc in range(KC)]
                        dstl = [xT_l[:, (sl * KC + kc) * 128:(sl * KC + kc + 1) * 128] for kc in range(KC)]
                        et = transpose_batch(srcs, dsth, dstl, xt, extra_wait=xT_read.get(sl))
                        xtm_read = p.tok("pe")
                        return et

                    def p3_P(g, xs_tok):
                        sl = g % 2
                        slices = []
                        for ds in range(8):
                            slices.append((cache["wq"][0], cache["wq"][1], ds * 256, BOFF["q"] + ds * 256,
                                           (lambda ti, ds=ds: q_sb[:, ti * D + ds * 256: ti * D + ds * 256 + 256]), None))
                        for ds in range(8):
                            slices.append((cache["wg"][0], cache["wg"][1], ds * 256, BOFF["g"] + ds * 256,
                                           (lambda ti, ds=ds: gate_sb[:, ti * D + ds * 256: ti * D + ds * 256 + 256]),
                                           AF.Sigmoid))
                        qt = qread.get(g - 2)
                        if qt is not None:
                            for t_ in qt:
                                p.wait(VEC, t_)
                                p.wait(ACT, t_)
                        mt = proj_pairs(slices, xs_tok,
                                        (lambda ti, kc: xT_h[:, (ti * KC + kc) * 128:(ti * KC + kc + 1) * 128]),
                                        (lambda ti, kc: xT_l[:, (ti * KC + kc) * 128:(ti * KC + kc + 1) * 128]),
                                        prev_used=proj_mm.get(g - 1), tis=[sl])
                        proj_mm[g] = mt
                        xT_read[sl] = mt
                        proj_done[g] = [p.tok("ve"), p.tok("se")]

                    def p3_C(g):
                        nonlocal xtm_free, y1T_used, y1T_store_tok
                        ti = g % 2
                        tc = g
                        p.wait(SYN, xtm_read)
                        if xtm_free is not None:
                            for t_ in xtm_free:
                                p.wait(SYN, t_)
                        p.wait(SYN, cdpfp.get(g - 1))
                        p.wait(SYN, qread.get(g - 1) and qread[g - 1][1])
                        t = SYN.dma_start(out=x_tm[:], in_=x_d.ap()[tc * 128:(tc + 1) * 128, :])
                        x_tok = p.inc(t, "lw", 16)

                        p.wait_all(ACT, proj_done[g])
                        p.wait_all(VEC, proj_done[g])
                        z_ve = None
                        for hg in range(8):
                            h0 = hg * 2
                            mkt = dpfp(aq_sb[:], mq_sb[:], 2,
                                       q_sb[:, ti * D + h0 * 128: ti * D + (h0 + 2) * 128],
                                       pe_wait=(cend_pe.get(g - 1) if hg == 0 else None),
                                       ve_wait=(cdpfp.get(g - 1) if hg == 0 else z_ve))
                            cdpfp[g] = mkt
                            nt = mk_normalize(mq_sb[:], 2, C_NSQA)
                            srcs = []
                            dsth = []
                            for j in range(2):
                                for fc in range(FCN):
                                    srcs.append(mq_sb[:, j * 768 + fc * 128: j * 768 + (fc + 1) * 128])
                                    dsth.append(mqT_sb[:, (j * FCN + fc) * 128:(j * FCN + fc + 1) * 128])
                            mqT_tok = transpose_batch(srcs, dsth, None, mkt)

                            tiles = [(psE1, 0), (psE1, 130)]
                            p.acquire(TEN, "psE1")
                            p.wait(TEN, mqT_tok)
                            last = None
                            for fc in range(FCN):
                                for j in range(2):
                                    h = h0 + j
                                    mm = TEN.matmul(psE1[:, tiles[j][1]:tiles[j][1] + 129],
                                                    mqT_sb[:, (j * FCN + fc) * 128:(j * FCN + fc + 1) * 128],
                                                    memn[:, (h * FCN + fc) * 129:(h * FCN + fc + 1) * 129],
                                                    start=(fc == 0 and j == 0),
                                                    stop=(fc == FCN - 1), skip_group_check=True)
                                    last = mm
                            mt = p.inc(last, "pe")
                            p.wait(VEC, mt)
                            for j in range(2):
                                VEC.tensor_copy(small[:, C_DENP + h0 + j:C_DENP + h0 + j + 1],
                                                psE1[:, j * 130 + 128:j * 130 + 129])
                            VEC.drain()
                            VEC.tensor_tensor(small[:, C_DENP + h0:C_DENP + h0 + 2],
                                              small[:, C_DENP + h0:C_DENP + h0 + 2],
                                              small[:, C_RINV:C_RINV + 2], op=ALU.mult)
                            VEC.drain()
                            VEC.tensor_scalar_add(small[:, C_DENP + h0:C_DENP + h0 + 2],
                                                  small[:, C_DENP + h0:C_DENP + h0 + 2], EPS_A)
                            VEC.drain()
                            VEC.reciprocal(small[:, C_RD + h0:C_RD + h0 + 2],
                                           small[:, C_DENP + h0:C_DENP + h0 + 2])
                            VEC.drain()
                            v = VEC.tensor_tensor(small[:, C_RDP + h0:C_RDP + h0 + 2],
                                                  small[:, C_RD + h0:C_RD + h0 + 2],
                                                  small[:, C_RINV:C_RINV + 2], op=ALU.mult)
                            vt = p.inc(v, "ve")
                            p.wait(ACT, vt)
                            lasts = None
                            for j in range(2):
                                h = h0 + j
                                lasts = ACT.activation(out=ttmp[:, j * 128:(j + 1) * 128],
                                                       in_=psE1[:, tiles[j][1]:tiles[j][1] + 128],
                                                       func=AF.Identity,
                                                       scale=small[:, C_RDP + h:C_RDP + h + 1])
                            st = p.inc(lasts, "se")
                            p.release("psE1", st)
                            p.wait(VEC, st)
                            p.wait(VEC, x_tok)
                            for j in range(2):
                                h = h0 + j
                                VEC.tensor_mul(ttmp3b[:, j * 128:(j + 1) * 128], ttmp[:, j * 128:(j + 1) * 128],
                                               gate_sb[:, ti * D + h * 128: ti * D + (h + 1) * 128])
                            VEC.drain()
                            lastv = None
                            for j in range(2):
                                h = h0 + j
                                lastv = VEC.scalar_tensor_tensor(
                                    out=x_tm[:, h * 128:(h + 1) * 128],
                                    in0=x_tm[:, h * 128:(h + 1) * 128], scalar=2.0,
                                    in1=ttmp3b[:, j * 128:(j + 1) * 128], op0=ALU.mult, op1=ALU.add)
                            z_ve = p.inc(lastv, "ve")
                        qread[g] = [p.tok("se"), p.tok("ve")]

                        lt = ln_chain(x_tm[:], x_tm[:])
                        p.wait(GPS, lt)
                        p.dma(GPS, "lg", y1_d.ap()[tc * 128:(tc + 1) * 128, :], x_tm[:])
                        if y1T_used is not None:
                            p.wait(VEC, y1T_used)
                        if y1T_store_tok is not None:
                            p.wait(VEC, y1T_store_tok)
                        srcs = [x_tm[:, kc * 128:(kc + 1) * 128] for kc in range(KC)]
                        dsth = [y1T_th[:, kc * 128:(kc + 1) * 128] for kc in range(KC)]
                        dstl = [y1T_tl[:, kc * 128:(kc + 1) * 128] for kc in range(KC)]
                        yt = transpose_batch(srcs, dsth, dstl, lt)
                        p.wait(GPS, yt)
                        p.dma(GPS, "lg", y1T_h_d.ap()[tc], y1T_th[:])
                        y1T_st = p.dma(GPS, "lg", y1T_l_d.ap()[tc], y1T_tl[:])
                        xtm_free = [p.tok("pe"), p.tok("lg")]
                        y1T_store_tok = y1T_st

                        p.acquire(TEN, "psG")
                        p.wait(TEN, yt)
                        last = None
                        for kc in range(KC):
                            yh = y1T_th[:, kc * 128:(kc + 1) * 128]
                            yl = y1T_tl[:, kc * 128:(kc + 1) * 128]
                            gh = gwT_h[:, kc * 8:(kc + 1) * 8]
                            gl = gwT_l[:, kc * 8:(kc + 1) * 8]
                            last = TEN.matmul(psG[0:8, 0:128], gh, yh, start=(kc == 0), stop=False)
                            last = TEN.matmul(psG[0:8, 0:128], gl, yh, start=False, stop=False)
                            last = TEN.matmul(psG[0:8, 0:128], gh, yl, start=False,
                                              stop=(kc == KC - 1))
                        mt = p.inc(last, "pe")
                        p.wait(VEC, mt)
                        v = VEC.tensor_copy(small[0:8, C_LGT:C_LGT + 128], psG[0:8, 0:128])
                        vt = p.inc(v, "ve")
                        p.wait(TEN, vt)
                        tr = TEN.transpose(psG[:, 128:136], small[0:8, C_LGT:C_LGT + 128],
                                           ident[0:8, 0:8])
                        mt = p.inc(tr, "pe")
                        p.wait(VEC, mt)
                        lg = small[:, C_LTM:C_LTM + 8]
                        v = VEC.tensor_copy(lg, psG[:, 128:136])
                        p.release("psG", p.inc(v, "ve"))
                        VEC.drain()
                        g8 = lambda i: small[:, C_G8 + i * 8:C_G8 + (i + 1) * 8]
                        max8, lz1, lz2, msk1, msk2, mr1, mr2 = (g8(i) for i in range(7))
                        VEC.max(out=max8, in_=lg)
                        VEC.memset(mr1, -1e30)
                        VEC.memset(mr2, -1e30)
                        VEC.drain()
                        VEC.tensor_copy(mr1[:, 0:1], max8[:, 0:1])
                        VEC.tensor_copy(mr2[:, 0:1], max8[:, 1:2])
                        VEC.drain()
                        VEC.match_replace(out=lz1, in_to_replace=mr1, in_values=lg, imm_value=-1e38)
                        VEC.drain()
                        VEC.match_replace(out=lz2, in_to_replace=mr2, in_values=lz1, imm_value=-1e38)
                        VEC.drain()
                        VEC.tensor_tensor(msk1, lz1, lg, op=ALU.not_equal)
                        VEC.tensor_tensor(msk2, lz2, lz1, op=ALU.not_equal)
                        v = VEC.tensor_sub(small[:, C_GS:C_GS + 1], max8[:, 0:1], max8[:, 1:2])
                        vt = p.inc(v, "ve")
                        p.wait(ACT, vt)
                        ACT.activation(out=small[:, C_GS + 1:C_GS + 2], in_=small[:, C_GS:C_GS + 1],
                                       func=AF.Sigmoid)
                        s = ACT.activation(out=small[:, C_GS + 2:C_GS + 3], in_=small[:, C_GS:C_GS + 1],
                                           func=AF.Sigmoid, scale=-1.0)
                        st = p.inc(s, "se")
                        p.wait(VEC, st)
                        VEC.tensor_scalar_mul(msk1, msk1, small[:, C_GS + 1:C_GS + 2])
                        VEC.tensor_scalar_mul(msk2, msk2, small[:, C_GS + 2:C_GS + 3])
                        VEC.drain()
                        v = VEC.tensor_add(cmb_all[:, tc * 8:(tc + 1) * 8], msk1, msk2)
                        p.inc(v, "ve")
                        y1T_used = p.tok("pe")
                        cend_pe[g] = p.tok("pe")

                    for g in range(NTC):
                        xs = p3_T(g)
                        p3_P(g, xs)
                        if g > 0:
                            p3_C(g - 1)
                    p3_C(NTC - 1)

                    p.full_barrier()

            FG = 2
            with nc.sbuf_tensor("y1Tg_h", [128, FG * D], BF16) as y1Tg_h, \
                 nc.sbuf_tensor("y1Tg_l", [128, FG * D], BF16) as y1Tg_l, \
                 nc.sbuf_tensor("sg_g", [128, FG * 512], F32) as sg_g, \
                 nc.sbuf_tensor("act_g", [128, FG * II], F32) as act_g, \
                 nc.sbuf_tensor("actT_h", [128, FG * II], BF16) as actT_h, \
                 nc.sbuf_tensor("actT_l", [128, FG * II], BF16) as actT_l, \
                 nc.sbuf_tensor("out_acc", [128, FG * D], F32) as out_acc, \
                 nc.sbuf_tensor("wt3_h", [128, 8192], BF16) as wt3_h, \
                 nc.sbuf_tensor("wt3_l", [128, 8192], BF16) as wt3_l, \
                 nc.sbuf_tensor("wt4_h", [128, 8192], BF16) as wt4_h, \
                 nc.sbuf_tensor("wt4_l", [128, 8192], BF16) as wt4_l, \
                 nc.sbuf_tensor("y1ld", [128, D], F32) as y1ld:

                outb = act_g
                wsets = [((wt1_h, wt1_l), (wt2_h, wt2_l)),
                         ((wt3_h, wt3_l), (wt4_h, wt4_l))]
                final_store = None
                y1g_used = None
                pending_final = None

                def emit_final(gf, otoks):
                    nonlocal final_store
                    for ti in range(FG):
                        tc = gf * FG + ti
                        if final_store is not None:
                            p.wait(SYN, final_store)
                        t = SYN.dma_start(out=y1ld[:], in_=y1_d.ap()[tc * 128:(tc + 1) * 128, :])
                        lt = p.inc(t, "lw", 16)
                        p.wait(VEC, lt)
                        for dsx in range(4):
                            p.wait(VEC, otoks.get((ti, dsx)))
                        v = VEC.tensor_add(y1ld[:], y1ld[:], out_acc[:, ti * D:(ti + 1) * D])
                        vt = p.inc(v, "ve")
                        if final_store is not None:
                            p.wait(ACT, final_store)
                        lnt = ln_chain(y1ld[:], y1ld[:])
                        p.wait(GPS, lnt)
                        t = GPS.dma_start(out=out_d.ap()[tc * 128:(tc + 1) * 128, :], in_=y1ld[:])
                        final_store = p.inc(t, "lg", 16)

                for g in range(NTC // FG):
                    if y1g_used is not None:
                        p.wait(SYN, y1g_used)
                    p.dma(SYN, "lw", y1Tg_h.ap().rearrange("p (t c) -> p t c", t=FG),
                          y1T_h_d.ap()[g * FG:(g + 1) * FG].rearrange("t p c -> p t c"))
                    y1g_tok = p.dma(SYN, "lw", y1Tg_l.ap().rearrange("p (t c) -> p t c", t=FG),
                                    y1T_l_d.ap()[g * FG:(g + 1) * FG].rearrange("t p c -> p t c"))

                    stages = []
                    for e in range(E):
                        for (c0, w) in W1_SLICES:
                            stages.append(("w1", e, c0, w))
                        for dsp in range(2):
                            stages.append(("w2", e, dsp))
                    load_toks = {}
                    mm_toks = {}

                    def emit_ffn_load(si):
                        kind, e, a1, a2 = stages[si][0], stages[si][1], stages[si][2], stages[si][3] if len(stages[si]) > 3 else None
                        (b1h, b1l), (b2h, b2l) = wsets[si % 2]
                        if si >= 2 and (si - 2) in mm_toks:
                            p.wait(SYN, mm_toks[si - 2])
                        if kind == "w1":
                            c0, w = a1, a2
                            last = None
                            for kc in range(KC):
                                p.dma(SYN, "lw", b1h[:, kc * 512:kc * 512 + w],
                                      cache["w1"][0].ap()[e, kc * 128:(kc + 1) * 128, c0:c0 + w])
                                p.dma(SYN, "lw", b1l[:, kc * 512:kc * 512 + w],
                                      cache["w1"][1].ap()[e, kc * 128:(kc + 1) * 128, c0:c0 + w])
                                p.dma(SYN, "lw", b2h[:, kc * 512:kc * 512 + w],
                                      cache["w1"][0].ap()[e, kc * 128:(kc + 1) * 128, 1408 + c0:1408 + c0 + w])
                                last = p.dma(SYN, "lw", b2l[:, kc * 512:kc * 512 + w],
                                             cache["w1"][1].ap()[e, kc * 128:(kc + 1) * 128, 1408 + c0:1408 + c0 + w])
                        else:
                            dsp = a1
                            ds0, ds1 = 2 * dsp, 2 * dsp + 1
                            last = None
                            for ic in range(ICN):
                                p.dma(SYN, "lw", b1h[:, ic * 512:(ic + 1) * 512],
                                      cache["w2"][0].ap()[e, ic * 128:(ic + 1) * 128, ds0 * 512:(ds0 + 1) * 512])
                                p.dma(SYN, "lw", b1l[:, ic * 512:(ic + 1) * 512],
                                      cache["w2"][1].ap()[e, ic * 128:(ic + 1) * 128, ds0 * 512:(ds0 + 1) * 512])
                                p.dma(SYN, "lw", b2h[:, ic * 512:(ic + 1) * 512],
                                      cache["w2"][0].ap()[e, ic * 128:(ic + 1) * 128, ds1 * 512:(ds1 + 1) * 512])
                                last = p.dma(SYN, "lw", b2l[:, ic * 512:(ic + 1) * 512],
                                             cache["w2"][1].ap()[e, ic * 128:(ic + 1) * 128, ds1 * 512:(ds1 + 1) * 512])
                        load_toks[si] = last

                    oacc_toks = {}
                    act_tok = None
                    aT_tok = None
                    w2mm_tok = None
                    sg_read = {}
                    emit_ffn_load(0)
                    for si, st_ in enumerate(stages):
                        if si + 1 < len(stages):
                            emit_ffn_load(si + 1)
                        kind, e = st_[0], st_[1]
                        (b1h, b1l), (b2h, b2l) = wsets[si % 2]
                        lt = load_toks[si]
                        if kind == "w1":
                            c0, w = st_[2], st_[3]
                            for ti in range(FG):
                                tc = g * FG + ti
                                bks = [(psA, "psA"), (psB, "psB")] if ti % 2 == 0 else [(psE1, "psE1"), (psE2, "psE2")]
                                mt = mm3m(
                                    (lambda kc, ti=ti: y1Tg_h[:, (ti * KC + kc) * 128:(ti * KC + kc + 1) * 128]),
                                    (lambda kc, ti=ti: y1Tg_l[:, (ti * KC + kc) * 128:(ti * KC + kc + 1) * 128]),
                                    [(bks[0][0], bks[0][1],
                                      (lambda kc: b1h[:, kc * 512:kc * 512 + w]),
                                      (lambda kc: b1l[:, kc * 512:kc * 512 + w]), w, None),
                                     (bks[1][0], bks[1][1],
                                      (lambda kc: b2h[:, kc * 512:kc * 512 + w]),
                                      (lambda kc: b2l[:, kc * 512:kc * 512 + w]), w, None)],
                                    KC, waits=[y1g_tok, lt])
                                mm_toks[si] = mt
                                p.wait(ACT, mt)
                                if sg_read.get(ti) is not None:
                                    p.wait(ACT, sg_read[ti])
                                s = ACT.activation(out=sg_g[:, ti * 512:ti * 512 + w],
                                                   in_=bks[0][0][:, 0:w], func=AF.Silu)
                                stk = p.inc(s, "se")
                                p.release(bks[0][1], stk)
                                p.wait(VEC, stk)
                                v = VEC.scalar_tensor_tensor(
                                    out=act_g[:, ti * II + c0: ti * II + c0 + w],
                                    in0=bks[1][0][:, 0:w], scalar=cmb_all[:, tc * 8 + e:tc * 8 + e + 1],
                                    in1=sg_g[:, ti * 512:ti * 512 + w],
                                    op0=ALU.mult, op1=ALU.mult)
                                vt = p.inc(v, "ve")
                                p.release(bks[1][1], vt)
                                sg_read[ti] = vt
                                act_tok = vt
                            if st_[2] == W1_SLICES[-1][0]:
                                for ti in range(FG):
                                    if w2mm_tok is not None:
                                        p.wait(VEC, w2mm_tok)
                                    srcs = [act_g[:, ti * II + ic * 128: ti * II + (ic + 1) * 128] for ic in range(ICN)]
                                    dsth = [actT_h[:, (ti * ICN + ic) * 128:(ti * ICN + ic + 1) * 128] for ic in range(ICN)]
                                    dstl = [actT_l[:, (ti * ICN + ic) * 128:(ti * ICN + ic + 1) * 128] for ic in range(ICN)]
                                    aT_tok = transpose_batch(srcs, dsth, dstl, act_tok)
                        else:
                            dsp = st_[2]
                            ds0, ds1 = 2 * dsp, 2 * dsp + 1
                            for ti in range(FG):
                                bks = [(psA, "psA"), (psB, "psB")] if ti % 2 == 0 else [(psE1, "psE1"), (psE2, "psE2")]
                                mt = mm3m(
                                    (lambda ic, ti=ti: actT_h[:, (ti * ICN + ic) * 128:(ti * ICN + ic + 1) * 128]),
                                    (lambda ic, ti=ti: actT_l[:, (ti * ICN + ic) * 128:(ti * ICN + ic + 1) * 128]),
                                    [(bks[0][0], bks[0][1],
                                      (lambda ic: b1h[:, ic * 512:(ic + 1) * 512]),
                                      (lambda ic: b1l[:, ic * 512:(ic + 1) * 512]), 512, None),
                                     (bks[1][0], bks[1][1],
                                      (lambda ic: b2h[:, ic * 512:(ic + 1) * 512]),
                                      (lambda ic: b2l[:, ic * 512:(ic + 1) * 512]), 512, None)],
                                    ICN, waits=[aT_tok, lt])
                                mm_toks[si] = mt
                                w2mm_tok = mt
                                for i, ds in enumerate((ds0, ds1)):
                                    bank, bn = bks[i]
                                    p.wait(VEC, mt)
                                    dsl = out_acc[:, ti * D + ds * 512: ti * D + (ds + 1) * 512]
                                    if e == 0:
                                        ev = VEC.tensor_copy(dsl, bank[:, 0:512])
                                    else:
                                        k = oacc_toks.get((ti, ds))
                                        if k is not None:
                                            p.wait(VEC, k)
                                        ev = VEC.tensor_add(dsl, dsl, bank[:, 0:512])
                                    et = p.inc(ev, "ve")
                                    oacc_toks[(ti, ds)] = et
                                    p.release(bn, et)

                    emit_final(g, dict(oacc_toks))
                    y1g_used = p.tok("pe")

                p.wait(GPS, final_store)

    es_sem.close()
    es_ps.close()
    es_perm.close()
    return nc


_NC_CACHE = None


def kernel(**inputs):
    global _NC_CACHE
    x = np.asarray(inputs["L0_attn_hidden_states"], dtype=np.float32).reshape(TOK * NCORES, D)
    Wq = np.asarray(inputs["L0_attn_Wq"], dtype=np.float32)
    bq = np.asarray(inputs["L0_attn_bq"], dtype=np.float32)
    Wk = np.asarray(inputs["L0_attn_Wk"], dtype=np.float32)
    bk = np.asarray(inputs["L0_attn_bk"], dtype=np.float32)
    Wv = np.asarray(inputs["L0_attn_Wv"], dtype=np.float32)
    bv = np.asarray(inputs["L0_attn_bv"], dtype=np.float32)
    Wg = np.asarray(inputs["L0_attn_Wg"], dtype=np.float32)
    bg = np.asarray(inputs["L0_attn_bg"], dtype=np.float32)
    Wmb = np.asarray(inputs["L0_attn_Wmb"], dtype=np.float32)
    bmb = np.asarray(inputs["L0_attn_bmb"], dtype=np.float32)
    memory = np.asarray(inputs["L0_attn_memory"], dtype=np.float32)
    norm = np.asarray(inputs["L0_attn_norm"], dtype=np.float32)
    gate_w = np.asarray(inputs["L0_ffn_gate_w"], dtype=np.float32)
    w1 = np.asarray(inputs["L0_ffn_w1"], dtype=np.float32)
    w2 = np.asarray(inputs["L0_ffn_w2"], dtype=np.float32)

    WqT = np.ascontiguousarray(Wq.T)
    WkT = np.ascontiguousarray(Wk.T)
    WvT = np.ascontiguousarray(Wv.T)
    WgT = np.ascontiguousarray(Wg.T)
    WmbT = np.ascontiguousarray(Wmb.T)
    gwT = np.ascontiguousarray(gate_w.T)
    w1T = np.ascontiguousarray(np.transpose(w1, (0, 2, 1)))
    w2T = np.ascontiguousarray(np.transpose(w2, (0, 2, 1)))
    mem_r = memory.reshape(H, FCN, 128, DH).transpose(2, 0, 1, 3)
    nrm_r = norm.reshape(H, FCN, 128).transpose(2, 0, 1)[..., None]
    memn = np.ascontiguousarray(
        np.concatenate([mem_r, nrm_r], axis=-1).reshape(128, MEMN_W).astype(np.float32))
    bpack = np.concatenate([bq, bk, bv, bg, bmb]).reshape(1, 7168).astype(np.float32)
    ident = np.eye(128, dtype=np.float32)

    if _NC_CACHE is None:
        _NC_CACHE = build_core()
    nc = _NC_CACHE

    shared = {"WqT": WqT, "WkT": WkT, "WvT": WvT, "WgT": WgT, "WmbT": WmbT,
              "gwT": gwT, "w1T": w1T, "w2T": w2T, "memn": memn,
              "bpack": bpack, "ident": ident}
    in_maps = [dict(shared, x=np.ascontiguousarray(x[c * TOK:(c + 1) * TOK]))
               for c in range(NCORES)]
    res = run_bass_kernel_spmd(nc, in_maps, core_ids=list(range(NCORES)))
    out = np.concatenate([r["out"] for r in res.results], axis=0)
    return out.reshape(2, 8192, 2048).astype(np.float32)


if __name__ == "__main__":
    pass


# version 13
# speedup vs baseline: 1.0091x; 1.0091x over previous
import copy as _copy
import numpy as np
from contextlib import ExitStack

import concourse.bass as bass
import concourse.mybir as mybir
from concourse.bass_utils import run_bass_kernel_spmd

F32 = mybir.dt.float32
BF16 = mybir.dt.bfloat16
AF = mybir.ActivationFunctionType
ALU = mybir.AluOpType

NCORES = 8
TOK = 2048
NTC = 16
TCG = 2
NG = NTC // TCG
D = 2048
KC = 16
H, HKV, DH = 16, 4, 128
FD, FCN = 768, 6
E, II, GUD = 8, 1408, 2816
ICN = 11
EPS_A = 1e-8
EPS_LN = 1e-5

W1_SLICES = [(0, 512), (512, 512), (1024, 384)]
MEMN_W = H * FCN * 129


class P:

    def __init__(self, nc, es):
        self.nc = nc
        names = ["lw", "lg", "pe", "ve", "se", "cc"]
        self.sem = {n: es.enter_context(nc.semaphore(n)) for n in names}
        self.cnt = {n: 0 for n in names}
        self.bank_free = {}

    def inc(self, instr, s, k=1):
        self.cnt[s] += k
        instr.then_inc(self.sem[s], k)
        return (s, self.cnt[s])

    def tok(self, s):
        return (s, self.cnt[s])

    def wait(self, eng, t):
        if t is not None and t[1] > 0:
            eng.wait_ge(self.sem[t[0]], t[1])

    def wait_all(self, eng, toks):
        for t in toks:
            self.wait(eng, t)

    def acquire(self, eng, bank):
        t = self.bank_free.get(bank)
        if t is not None:
            self.wait(eng, t)
            self.bank_free[bank] = None

    def release(self, bank, t):
        self.bank_free[bank] = t

    def full_barrier(self):
        engs = [self.nc.tensor, self.nc.vector, self.nc.scalar,
                self.nc.gpsimd, self.nc.sync]
        for e in engs:
            for s in ("lw", "lg", "pe", "ve", "se", "cc"):
                if self.cnt[s] > 0:
                    e.wait_ge(self.sem[s], self.cnt[s])

    def guard(self):
        nc = self.nc
        for s in ("pe", "se"):
            if self.cnt[s] > 0:
                nc.vector.wait_ge(self.sem[s], self.cnt[s])
        for s in ("pe", "ve"):
            if self.cnt[s] > 0:
                nc.scalar.wait_ge(self.sem[s], self.cnt[s])
        for s in ("ve", "se"):
            if self.cnt[s] > 0:
                nc.tensor.wait_ge(self.sem[s], self.cnt[s])

    def guard_syn(self):
        for s in ("pe", "ve", "se", "lg"):
            if self.cnt[s] > 0:
                self.nc.sync.wait_ge(self.sem[s], self.cnt[s])

    def dma(self, eng, s, out, in_, **kw):
        t = eng.dma_start(out=out, in_=in_, **kw)
        return self.inc(t, s, 16)


def build_core():
    nc = bass.Bass("TRN2", target_bir_lowering=False)

    dt = lambda n, s, k=None: (nc.dram_tensor(n, s, F32, kind=k) if k else
                               nc.dram_tensor(n, s, F32))
    x_d = nc.dram_tensor("x", [TOK, D], F32, kind="ExternalInput")
    wqT_d = nc.dram_tensor("WqT", [D, D], F32, kind="ExternalInput")
    wkT_d = nc.dram_tensor("WkT", [D, 512], F32, kind="ExternalInput")
    wvT_d = nc.dram_tensor("WvT", [D, 512], F32, kind="ExternalInput")
    wgT_d = nc.dram_tensor("WgT", [D, D], F32, kind="ExternalInput")
    wmbT_d = nc.dram_tensor("WmbT", [D, D], F32, kind="ExternalInput")
    gwT_d = nc.dram_tensor("gwT", [D, 8], F32, kind="ExternalInput")
    w1T_d = nc.dram_tensor("w1T", [E, D, GUD], F32, kind="ExternalInput")
    w2T_d = nc.dram_tensor("w2T", [E, II, D], F32, kind="ExternalInput")
    memn_d = nc.dram_tensor("memn", [128, MEMN_W], F32, kind="ExternalInput")
    bias_d = nc.dram_tensor("bpack", [1, 7168], F32, kind="ExternalInput")
    ident_d = nc.dram_tensor("ident", [128, 128], F32, kind="ExternalInput")
    out_d = nc.dram_tensor("out", [TOK, D], F32, kind="ExternalOutput")

    cache = {}
    for name, shape in [("wq", [D, D]), ("wk", [D, 512]), ("wv", [D, 512]),
                        ("wg", [D, D]), ("wmb", [D, D]),
                        ("w1", [E, D, GUD]), ("w2", [E, II, D])]:
        cache[name] = (nc.dram_tensor(f"c_{name}_h", shape, BF16),
                       nc.dram_tensor(f"c_{name}_l", shape, BF16))
    delta_d = nc.dram_tensor("delta", [128, MEMN_W], F32)
    ccout_d = nc.dram_tensor("ccout", [128, MEMN_W], F32)
    y1_d = nc.dram_tensor("y1", [TOK, D], F32)
    y1T_h_d = nc.dram_tensor("y1T_h", [NTC, 128, D], BF16)
    y1T_l_d = nc.dram_tensor("y1T_l", [NTC, 128, D], BF16)

    BOFF = {"q": 0, "k": 2048, "v": 2560, "g": 3072, "mb": 5120}

    es_perm = ExitStack()
    ident = es_perm.enter_context(nc.sbuf_tensor("ident_sb", [128, 128], F32))
    ones = es_perm.enter_context(nc.sbuf_tensor("ones_sb", [128, 128], F32))
    bias_sb = es_perm.enter_context(nc.sbuf_tensor("bias_sb", [1, 1024], F32))
    small = es_perm.enter_context(nc.sbuf_tensor("small_sb", [128, 448], F32))
    gwT_h = es_perm.enter_context(nc.sbuf_tensor("gwT_h", [128, 128], BF16))
    gwT_l = es_perm.enter_context(nc.sbuf_tensor("gwT_l", [128, 128], BF16))
    cmb_all = es_perm.enter_context(nc.sbuf_tensor("cmb_all", [128, 128], F32))
    wt1_h = es_perm.enter_context(nc.sbuf_tensor("wt1_h", [128, 8192], BF16))
    wt1_l = es_perm.enter_context(nc.sbuf_tensor("wt1_l", [128, 8192], BF16))
    wt2_h = es_perm.enter_context(nc.sbuf_tensor("wt2_h", [128, 8192], BF16))
    wt2_l = es_perm.enter_context(nc.sbuf_tensor("wt2_l", [128, 8192], BF16))

    C_NSQA, C_NSQB, C_S, C_RINV, C_MKSQ, C_RMKSQ = 0, 4, 8, 12, 16, 20
    C_DENP, C_COEF, C_RD = 24, 40, 56
    C_RDP = 320
    C_LN = 72
    C_GS = 88
    C_EPS = 96
    C_G8 = 104
    C_LGT = 160
    C_LTM = 288

    es_ps = ExitStack()
    psA = es_ps.enter_context(nc.psum_tensor("psA", [128, 512], F32))
    psB = es_ps.enter_context(nc.psum_tensor("psB", [128, 512], F32))
    psT1 = es_ps.enter_context(nc.psum_tensor("psT1", [128, 512], F32))
    psT2 = es_ps.enter_context(nc.psum_tensor("psT2", [128, 512], F32))
    psE1 = es_ps.enter_context(nc.psum_tensor("psE1", [128, 512], F32))
    psE2 = es_ps.enter_context(nc.psum_tensor("psE2", [128, 512], F32))
    psG = es_ps.enter_context(nc.psum_tensor("psG", [128, 512], F32))
    psX = es_ps.enter_context(nc.psum_tensor("psX", [128, 512], F32))

    es_sem = ExitStack()

    with nc.Block() as block:

        @block.sync
        def _(sync):
            p = P(nc, es_sem)
            TEN, VEC, ACT, GPS, SYN = nc.tensor, nc.vector, nc.scalar, nc.gpsimd, nc.sync

            def transpose_batch(srcs, dsts_h, dsts_l, src_tok, extra_wait=None):
                ev_tok = None
                for i0 in range(0, len(srcs), 4):
                    batch = srcs[i0:i0 + 4]
                    bank = psT1 if (i0 // 4) % 2 == 0 else psT2
                    bname = "psT1" if bank is psT1 else "psT2"
                    p.wait(TEN, src_tok)
                    if extra_wait is not None:
                        p.wait(TEN, extra_wait)
                    p.acquire(TEN, bname)
                    last = None
                    for j, s in enumerate(batch):
                        last = TEN.transpose(bank[:, j * 128:(j + 1) * 128], s, ident[:])
                    ptok = p.inc(last, "pe")
                    p.wait(VEC, ptok)
                    lastv = None
                    for j in range(len(batch)):
                        lastv = VEC.tensor_copy(dsts_h[i0 + j], bank[:, j * 128:(j + 1) * 128])
                    if dsts_l is not None:
                        for j in range(len(batch)):
                            lastv = VEC.tensor_sub(dsts_l[i0 + j], bank[:, j * 128:(j + 1) * 128], dsts_h[i0 + j])
                    ev_tok = p.inc(lastv, "ve")
                    p.release(bname, ev_tok)
                return ev_tok

            def load_wT_slice(hi_d, lo_d, k0, nkc, c0, w, dst_h, dst_l, dst_c0, stride):
                t = None
                for g in range(0, nkc, 4):
                    n = min(4, nkc - g)
                    src_h = hi_d.ap()[k0 + g * 128:k0 + (g + n) * 128, c0:c0 + w].rearrange(
                        "(kc p) w -> p kc w", p=128)
                    src_l = lo_d.ap()[k0 + g * 128:k0 + (g + n) * 128, c0:c0 + w].rearrange(
                        "(kc p) w -> p kc w", p=128)
                    dh = dst_h.ap()[:, dst_c0 + g * stride: dst_c0 + (g + n) * stride].rearrange(
                        "p (kc w) -> p kc w", kc=n)
                    dl = dst_l.ap()[:, dst_c0 + g * stride: dst_c0 + (g + n) * stride].rearrange(
                        "p (kc w) -> p kc w", kc=n)
                    p.dma(SYN, "lw", dh, src_h)
                    t = p.dma(SYN, "lw", dl, src_l)
                return t

            def mm3(bank, bank_name, xh_tiles, xl_tiles, wh_ap_fn, wl_ap_fn, w, nkc,
                    seed=None, waits=()):
                p.acquire(TEN, bank_name)
                for t in waits:
                    p.wait(TEN, t)
                first = True
                last = None
                if seed is not None:
                    last = TEN.matmul(bank[:, 0:w], ones[0:1, 0:128], seed,
                                      start=True, stop=False)
                    first = False
                for kc in range(nkc):
                    l = kc == nkc - 1
                    last = TEN.matmul(bank[:, 0:w], xh_tiles(kc), wh_ap_fn(kc),
                                      start=first, stop=False)
                    first = False
                    last = TEN.matmul(bank[:, 0:w], xh_tiles(kc), wl_ap_fn(kc),
                                      start=False, stop=False)
                    last = TEN.matmul(bank[:, 0:w], xl_tiles(kc), wh_ap_fn(kc),
                                      start=False, stop=l)
                return p.inc(last, "pe")

            def noldw(m):
                inst = _copy.replace(m.ins, ldweights=False)
                nc.register_instruction(inst, overwrite=True)

            def mm3m(xh_fn, xl_fn, movings, nkc, waits=()):
                for (bank, bn, _, _, _, _) in movings:
                    p.acquire(TEN, bn)
                for t in waits:
                    p.wait(TEN, t)
                started = [False] * len(movings)
                last = None
                for i, (bank, bn, mvh, mvl, w, seed) in enumerate(movings):
                    if seed is not None:
                        last = TEN.matmul(bank[:, 0:w], ones[0:1, 0:128], seed,
                                          start=True, stop=False)
                        started[i] = True
                pend = []
                for kc in range(nkc):
                    l = kc == nkc - 1
                    first_in_run = True
                    for i, (bank, bn, mvh, mvl, w, seed) in enumerate(movings):
                        last = TEN.matmul(bank[:, 0:w], xh_fn(kc), mvh(kc),
                                          start=not started[i], stop=False)
                        started[i] = True
                        if not first_in_run:
                            pend.append(last)
                        first_in_run = False
                        last = TEN.matmul(bank[:, 0:w], xh_fn(kc), mvl(kc),
                                          start=False, stop=False)
                        pend.append(last)
                    first_in_run = True
                    for i, (bank, bn, mvh, mvl, w, seed) in enumerate(movings):
                        last = TEN.matmul(bank[:, 0:w], xl_fn(kc), mvh(kc),
                                          start=False, stop=l)
                        if not first_in_run:
                            pend.append(last)
                        first_in_run = False
                tok = p.inc(last, "pe")
                for m in pend:
                    noldw(m)
                return tok

            def proj_pairs(slices, xs_tok, xh_of, xl_of, prev_used=None, tis=None):
                W = 256
                SL = KC * W
                if tis is None:
                    tis = list(range(TCG))
                sets = [(wt1_h, wt1_l, 0), (wt2_h, wt2_l, 512)]
                ngrp = (len(slices) + 1) // 2
                load_toks = {}
                mm_toks = {}

                def emit_load(gi):
                    hi_b, lo_b, bcol = sets[gi % 2]
                    if gi >= 2 and (gi - 2) in mm_toks:
                        p.wait(SYN, mm_toks[gi - 2])
                    elif gi < 2 and prev_used is not None:
                        p.wait(SYN, prev_used)
                    toks = []
                    for j, (hi_d, lo_d, c0, boff, dstf, act) in enumerate(slices[gi * 2: gi * 2 + 2]):
                        lt = load_wT_slice(hi_d, lo_d, 0, KC, c0, W, hi_b, lo_b, j * SL, W)
                        bt = p.dma(SYN, "lw", bias_sb[0:1, bcol + j * W: bcol + (j + 1) * W],
                                   bias_d.ap()[0:1, boff:boff + W])
                        toks += [lt, bt]
                    load_toks[gi] = toks

                emit_load(0)
                for gi in range(ngrp):
                    if gi + 1 < ngrp:
                        emit_load(gi + 1)
                    hi_b, lo_b, bcol = sets[gi % 2]
                    grp = slices[gi * 2: gi * 2 + 2]
                    for ti in tis:
                        bks = [(psA, "psA"), (psB, "psB")]
                        movings = []
                        for j in range(len(grp)):
                            movings.append((bks[j][0], bks[j][1],
                                            (lambda kc, j=j: hi_b[:, j * SL + kc * W: j * SL + (kc + 1) * W]),
                                            (lambda kc, j=j: lo_b[:, j * SL + kc * W: j * SL + (kc + 1) * W]),
                                            W, bias_sb[0:1, bcol + j * W: bcol + (j + 1) * W]))
                        mt = mm3m((lambda kc, ti=ti: xh_of(ti, kc)),
                                  (lambda kc, ti=ti: xl_of(ti, kc)),
                                  movings, KC, waits=[xs_tok] + load_toks[gi])
                        mm_toks[gi] = mt
                        for j, (hi_d, lo_d, c0, boff, dstf, act) in enumerate(grp):
                            bank, bn = bks[j]
                            dsl = dstf(ti)
                            if act is None:
                                p.wait(VEC, mt)
                                ev = VEC.tensor_copy(dsl, bank[:, 0:W])
                                et = p.inc(ev, "ve")
                            else:
                                p.wait(ACT, mt)
                                ev = ACT.activation(out=dsl, in_=bank[:, 0:W], func=act)
                                et = p.inc(ev, "se")
                            p.release(bn, et)
                return mm_toks[ngrp - 1]

            def ln_chain(z_ap, out_ap):
                c = C_LN
                zv = p.tok("ve")
                p.wait(ACT, zv)
                VEC.drain()
                t1 = VEC.reduce_sum(small[:, c:c + 1], z_ap, axis=mybir.AxisListType.X)
                vt = p.inc(t1, "ve")
                p.acquire(ACT, "psX")
                lasts = None
                for i in range(4):
                    lasts = ACT.activation(out=psX[:, 0:512], in_=z_ap[:, i * 512:(i + 1) * 512],
                                           func=AF.Square, accum_out=small[:, c + 1 + i:c + 2 + i])
                st = p.inc(lasts, "se")
                p.release("psX", st)
                p.wait(VEC, st)
                v = VEC.reduce_sum(small[:, c + 5:c + 6], small[:, c + 1:c + 5], axis=mybir.AxisListType.X)
                VEC.drain()
                VEC.tensor_scalar_mul(small[:, c + 6:c + 7], small[:, c:c + 1], 1.0 / D)
                VEC.tensor_scalar_mul(small[:, c + 7:c + 8], small[:, c + 5:c + 6], 1.0 / D)
                VEC.drain()
                v = VEC.tensor_mul(small[:, c + 8:c + 9], small[:, c + 6:c + 7], small[:, c + 6:c + 7])
                VEC.drain()
                v = VEC.tensor_sub(small[:, c + 8:c + 9], small[:, c + 7:c + 8], small[:, c + 8:c + 9])
                vt = p.inc(v, "ve")
                p.wait(ACT, vt)
                s = ACT.activation(out=small[:, c + 9:c + 10], in_=small[:, c + 8:c + 9],
                                   func=AF.Sqrt, bias=small[:, C_EPS:C_EPS + 1])
                st = p.inc(s, "se")
                p.wait(VEC, st)
                v = VEC.reciprocal(small[:, c + 9:c + 10], small[:, c + 9:c + 10])
                VEC.drain()
                v = VEC.scalar_tensor_tensor(out=small[:, c + 10:c + 11], in0=small[:, c + 6:c + 7],
                                             scalar=-1.0, in1=small[:, c + 9:c + 10],
                                             op0=ALU.mult, op1=ALU.mult)
                vt = p.inc(v, "ve")
                p.wait(ACT, vt)
                s = ACT.activation(out=out_ap, in_=z_ap, func=AF.Identity,
                                   scale=small[:, c + 9:c + 10], bias=small[:, c + 10:c + 11])
                return p.inc(s, "se")

            def dpfp(a_ap, mkr_ap, nheads, k_src_ap, pe_wait=None, ve_wait=None):
                if pe_wait is not None:
                    p.wait(VEC, pe_wait)
                    p.wait(ACT, pe_wait)
                if ve_wait is not None:
                    p.wait(ACT, ve_wait)
                lasts = None
                for j in range(nheads):
                    lasts = ACT.activation(out=a_ap[:, j * 256:j * 256 + 128],
                                           in_=k_src_ap[:, j * 128:(j + 1) * 128], func=AF.Relu)
                    lasts = ACT.activation(out=a_ap[:, j * 256 + 128:j * 256 + 256],
                                           in_=k_src_ap[:, j * 128:(j + 1) * 128], func=AF.Relu,
                                           scale=-1.0)
                at = p.inc(lasts, "se")
                p.wait(VEC, at)
                lastv = None
                for j in range(nheads):
                    ab = a_ap[:, j * 256:(j + 1) * 256]
                    ob = mkr_ap[:, j * 768:(j + 1) * 768]
                    for r in (1, 2, 3):
                        o0 = (r - 1) * 256
                        lastv = VEC.tensor_mul(ob[:, o0:o0 + r], ab[:, 0:r], ab[:, 256 - r:256])
                        lastv = VEC.tensor_mul(ob[:, o0 + r:o0 + 256], ab[:, r:256], ab[:, 0:256 - r])
                return p.inc(lastv, "ve")

            def mk_normalize(mkr_ap, nheads, c_nsq_base):
                p.wait(ACT, p.tok("ve"))
                p.acquire(ACT, "psX")
                lasts = None
                for j in range(nheads):
                    b = mkr_ap[:, j * 768:(j + 1) * 768]
                    lasts = ACT.activation(out=psX[:, 0:512], in_=b[:, 0:512], func=AF.Square,
                                           accum_out=small[:, C_NSQA + j:C_NSQA + j + 1])
                    lasts = ACT.activation(out=psX[:, 0:256], in_=b[:, 512:768], func=AF.Square,
                                           accum_out=small[:, C_NSQB + j:C_NSQB + j + 1])
                st = p.inc(lasts, "se")
                p.release("psX", st)
                p.wait(VEC, st)
                v = VEC.tensor_add(small[:, C_NSQA:C_NSQA + nheads],
                                   small[:, C_NSQA:C_NSQA + nheads],
                                   small[:, C_NSQB:C_NSQB + nheads])
                vt = p.inc(v, "ve")
                p.wait(ACT, vt)
                s = ACT.activation(out=small[:, C_S:C_S + nheads], in_=small[:, C_NSQA:C_NSQA + nheads],
                                   func=AF.Sqrt)
                st = p.inc(s, "se")
                p.wait(VEC, st)
                VEC.tensor_scalar_max(small[:, C_S:C_S + nheads], small[:, C_S:C_S + nheads], 1e-12)
                VEC.drain()
                VEC.reciprocal(small[:, C_RINV:C_RINV + nheads], small[:, C_S:C_S + nheads])
                VEC.drain()
                VEC.tensor_mul(small[:, C_MKSQ:C_MKSQ + nheads], small[:, C_NSQA:C_NSQA + nheads],
                               small[:, C_RINV:C_RINV + nheads])
                VEC.drain()
                VEC.tensor_mul(small[:, C_MKSQ:C_MKSQ + nheads], small[:, C_MKSQ:C_MKSQ + nheads],
                               small[:, C_RINV:C_RINV + nheads])
                VEC.drain()
                VEC.tensor_scalar_max(small[:, C_RMKSQ:C_RMKSQ + nheads],
                                      small[:, C_MKSQ:C_MKSQ + nheads], 1e-30)
                VEC.drain()
                v = VEC.reciprocal(small[:, C_RMKSQ:C_RMKSQ + nheads],
                                   small[:, C_RMKSQ:C_RMKSQ + nheads])
                return p.inc(v, "ve")

            t = SYN.dma_start(out=ident[:], in_=ident_d.ap())
            tok_id = p.inc(t, "lw", 16)
            VEC.memset(ones[:], 1.0)
            v = VEC.memset(small[:, C_EPS:C_EPS + 1], EPS_LN)
            tok_ones = p.inc(v, "ve")

            with nc.sbuf_tensor("memn_sb", [128, MEMN_W], F32) as memn, \
                 nc.sbuf_tensor("x_tm", [128, D], F32) as x_tm, \
                 nc.sbuf_tensor("xT_h", [128, TCG * D], BF16) as xT_h, \
                 nc.sbuf_tensor("xT_l", [128, TCG * D], BF16) as xT_l:

                t = SYN.dma_start(out=memn[:], in_=memn_d.ap())
                tok_memn = p.inc(t, "lw", 16)

                with nc.sbuf_tensor("gwf", [128, 128], F32) as gwf:
                    for kc in range(KC):
                        t = p.dma(SYN, "lw", gwf[:, kc * 8:(kc + 1) * 8],
                                  gwT_d.ap()[kc * 128:(kc + 1) * 128, :])
                    p.wait(VEC, t)
                    VEC.tensor_copy(gwT_h[:], gwf[:])
                    VEC.drain()
                    v = VEC.tensor_sub(gwT_l[:], gwf[:], gwT_h[:])
                    p.inc(v, "ve")

                    v = VEC.memset(gwf[:], 0.0)
                    zt = p.inc(v, "ve")
                    p.wait(GPS, zt)
                    for i in range(0, MEMN_W, 128):
                        w = min(128, MEMN_W - i)
                        tok_dz = p.dma(GPS, "lg", delta_d.ap()[:, i:i + w], gwf[:, 0:w])
                    p.full_barrier()

                with nc.sbuf_tensor("stg0", [128, GUD], F32) as stg0, \
                     nc.sbuf_tensor("stg1", [128, GUD], F32) as stg1, \
                     nc.sbuf_tensor("sph0", [128, GUD], BF16) as sph0, \
                     nc.sbuf_tensor("sph1", [128, GUD], BF16) as sph1, \
                     nc.sbuf_tensor("spl0", [128, GUD], BF16) as spl0, \
                     nc.sbuf_tensor("spl1", [128, GUD], BF16) as spl1:
                    stg = [stg0, stg1]
                    sph = [sph0, sph1]
                    spl = [spl0, spl1]
                    band_split_tok = {}
                    band_store_tok = {}
                    bands = []
                    for nm, src, ktot, dout in [
                        ("wk", wkT_d, D, 512), ("wv", wvT_d, D, 512),
                        ("wmb", wmbT_d, D, D), ("wq", wqT_d, D, D),
                        ("wg", wgT_d, D, D)]:
                        for b in range(ktot // 128):
                            bands.append((nm, src.ap()[b * 128:(b + 1) * 128, :],
                                          cache[nm][0].ap()[b * 128:(b + 1) * 128, :],
                                          cache[nm][1].ap()[b * 128:(b + 1) * 128, :], dout))
                    for e in range(E):
                        for b in range(KC):
                            bands.append(("w1", w1T_d.ap()[e, b * 128:(b + 1) * 128, :],
                                          cache["w1"][0].ap()[e, b * 128:(b + 1) * 128, :],
                                          cache["w1"][1].ap()[e, b * 128:(b + 1) * 128, :], GUD))
                    for e in range(E):
                        for b in range(ICN):
                            bands.append(("w2", w2T_d.ap()[e, b * 128:(b + 1) * 128, :],
                                          cache["w2"][0].ap()[e, b * 128:(b + 1) * 128, :],
                                          cache["w2"][1].ap()[e, b * 128:(b + 1) * 128, :], D))

                    for bi, (nm, src, dsth, dstl, dout) in enumerate(bands):
                        s = bi % 2
                        if bi >= 2:
                            p.wait(SYN, band_split_tok[bi - 2])
                        t = SYN.dma_start(out=stg[s][:, 0:dout], in_=src)
                        lt = p.inc(t, "lw", 16)
                        p.wait(ACT, lt)
                        if bi >= 2:
                            p.wait(ACT, band_store_tok[bi - 2])
                        sc = ACT.activation(out=sph[s][:, 0:dout], in_=stg[s][:, 0:dout],
                                            func=AF.Copy)
                        ht = p.inc(sc, "se")
                        p.wait(VEC, ht)
                        if bi >= 2:
                            p.wait(VEC, band_store_tok[bi - 2])
                        v = VEC.tensor_sub(spl[s][:, 0:dout], stg[s][:, 0:dout], sph[s][:, 0:dout])
                        st = p.inc(v, "ve")
                        band_split_tok[bi] = st
                        p.wait(GPS, st)
                        p.dma(GPS, "lg", dsth, sph[s][:, 0:dout])
                        band_store_tok[bi] = p.dma(GPS, "lg", dstl, spl[s][:, 0:dout])

                    p.full_barrier()

                with nc.sbuf_tensor("kraw", [128, 2 * 512], F32) as kraw, \
                     nc.sbuf_tensor("v_sb", [128, 2 * 512], F32) as v_sb, \
                     nc.sbuf_tensor("mb_sb", [128, 2 * D], F32) as mb_sb, \
                     nc.sbuf_tensor("a_sb", [128, 1024], F32) as a_sb, \
                     nc.sbuf_tensor("mk_sb", [128, HKV * 768], F32) as mk_sb, \
                     nc.sbuf_tensor("mkT_sb", [128, HKV * 768], F32) as mkT_sb, \
                     nc.sbuf_tensor("rhsg", [128, H * 129], F32) as rhsg, \
                     nc.sbuf_tensor("dtmp", [128, 258], F32) as dtmp, \
                     nc.sbuf_tensor("ttmp", [128, 512], F32) as ttmp:

                    ttmp2 = a_sb

                    dtmp_tok = [None, None]
                    dslice_tok = {}
                    cend_pe = {}
                    cdpfp = {}
                    xT_read = {}
                    xtm_read = None
                    proj_done = {}
                    proj_mm = {}
                    kraw_read = {}

                    def p1_T(g):
                        nonlocal xtm_read
                        sl = g % 2
                        if xtm_read is not None:
                            p.wait(SYN, xtm_read)
                        t = SYN.dma_start(out=x_tm[:], in_=x_d.ap()[g * 128:(g + 1) * 128, :])
                        xt = p.inc(t, "lw", 16)
                        srcs = [x_tm[:, kc * 128:(kc + 1) * 128] for kc in range(KC)]
                        dsth = [xT_h[:, (sl * KC + kc) * 128:(sl * KC + kc + 1) * 128] for kc in range(KC)]
                        dstl = [xT_l[:, (sl * KC + kc) * 128:(sl * KC + kc + 1) * 128] for kc in range(KC)]
                        et = transpose_batch(srcs, dsth, dstl, xt, extra_wait=xT_read.get(sl))
                        xtm_read = p.tok("pe")
                        return et

                    def p1_P(g, xs_tok):
                        sl = g % 2
                        slices = []
                        for ds in range(2):
                            slices.append((cache["wk"][0], cache["wk"][1], ds * 256, BOFF["k"] + ds * 256,
                                           (lambda ti, ds=ds: kraw[:, ti * 512 + ds * 256: ti * 512 + ds * 256 + 256]), None))
                        for ds in range(2):
                            slices.append((cache["wv"][0], cache["wv"][1], ds * 256, BOFF["v"] + ds * 256,
                                           (lambda ti, ds=ds: v_sb[:, ti * 512 + ds * 256: ti * 512 + ds * 256 + 256]), None))
                        for ds in range(8):
                            slices.append((cache["wmb"][0], cache["wmb"][1], ds * 256, BOFF["mb"] + ds * 256,
                                           (lambda ti, ds=ds: mb_sb[:, ti * D + ds * 256: ti * D + ds * 256 + 256]),
                                           AF.Sigmoid))
                        kt = kraw_read.get(g - 2)
                        if kt is not None:
                            for t_ in kt:
                                p.wait(VEC, t_)
                                p.wait(ACT, t_)
                        mt = proj_pairs(slices, xs_tok,
                                        (lambda ti, kc: xT_h[:, (ti * KC + kc) * 128:(ti * KC + kc + 1) * 128]),
                                        (lambda ti, kc: xT_l[:, (ti * KC + kc) * 128:(ti * KC + kc + 1) * 128]),
                                        prev_used=proj_mm.get(g - 1), tis=[sl])
                        proj_mm[g] = mt
                        xT_read[sl] = mt
                        proj_done[g] = [p.tok("ve"), p.tok("se")]

                    def p1_C(g):
                        ti = g % 2
                        p.wait_all(ACT, proj_done[g])
                        p.wait_all(VEC, proj_done[g])
                        mkt = dpfp(a_sb[:], mk_sb[:], HKV, kraw[:, ti * 512:(ti + 1) * 512],
                                   pe_wait=cend_pe.get(g - 1),
                                   ve_wait=(kraw_read.get(g - 1) or [None, None])[1])
                        cdpfp[g] = mkt
                        nt = mk_normalize(mk_sb[:], HKV, C_NSQA)
                        srcs = []
                        dsth = []
                        for j in range(HKV):
                            for fc in range(FCN):
                                srcs.append(mk_sb[:, j * 768 + fc * 128: j * 768 + (fc + 1) * 128])
                                dsth.append(mkT_sb[:, (j * FCN + fc) * 128:(j * FCN + fc + 1) * 128])
                        mkT_tok = transpose_batch(srcs, dsth, None, nt)

                        for kv in range(HKV):
                            tiles = [(psE1, "psE1", i * 130) for i in range(3)] + [(psE2, "psE2", 0)]
                            p.acquire(TEN, "psE1")
                            p.acquire(TEN, "psE2")
                            p.wait(TEN, mkT_tok)
                            p.wait(TEN, tok_memn)
                            last = None
                            for fc in range(FCN):
                                for j in range(4):
                                    h = kv * 4 + j
                                    bank, bn, off = tiles[j]
                                    mm = TEN.matmul(bank[:, off:off + 129],
                                                    mkT_sb[:, (kv * FCN + fc) * 128:(kv * FCN + fc + 1) * 128],
                                                    memn[:, (h * FCN + fc) * 129:(h * FCN + fc + 1) * 129],
                                                    start=(fc == 0 and j in (0, 3)),
                                                    stop=(fc == FCN - 1), skip_group_check=True)
                                    if j > 0:
                                        noldw(mm)
                                    last = mm
                            mt = p.inc(last, "pe")
                            h0 = kv * 4
                            p.wait(VEC, mt)
                            for j in range(3):
                                VEC.tensor_copy(small[:, C_DENP + h0 + j:C_DENP + h0 + j + 1],
                                                psE1[:, j * 130 + 128:j * 130 + 129])
                            VEC.tensor_copy(small[:, C_DENP + h0 + 3:C_DENP + h0 + 4], psE2[:, 128:129])
                            VEC.drain()
                            VEC.tensor_scalar(small[:, C_DENP + h0:C_DENP + h0 + 4],
                                              small[:, C_DENP + h0:C_DENP + h0 + 4],
                                              small[:, C_RINV + kv:C_RINV + kv + 1], EPS_A,
                                              op0=ALU.mult, op1=ALU.add)
                            VEC.drain()
                            VEC.reciprocal(small[:, C_RD + h0:C_RD + h0 + 4],
                                           small[:, C_DENP + h0:C_DENP + h0 + 4])
                            VEC.drain()
                            v = VEC.tensor_scalar_mul(small[:, C_RDP + h0:C_RDP + h0 + 4],
                                                      small[:, C_RD + h0:C_RD + h0 + 4],
                                                      small[:, C_RINV + kv:C_RINV + kv + 1])
                            vt = p.inc(v, "ve")
                            p.wait(ACT, vt)
                            lasts = None
                            for j in range(4):
                                h = h0 + j
                                bank, bn, off = tiles[j]
                                lasts = ACT.activation(out=ttmp[:, j * 128:(j + 1) * 128],
                                                       in_=bank[:, off:off + 128], func=AF.Identity,
                                                       scale=small[:, C_RDP + h:C_RDP + h + 1])
                            st = p.inc(lasts, "se")
                            p.release("psE1", st)
                            p.release("psE2", st)
                            p.wait(VEC, st)
                            lastv = None
                            for j in range(4):
                                h = h0 + j
                                lastv = VEC.tensor_sub(
                                    ttmp2[:, j * 128:(j + 1) * 128],
                                    v_sb[:, ti * 512 + kv * 128: ti * 512 + (kv + 1) * 128],
                                    ttmp[:, j * 128:(j + 1) * 128])
                            for j in range(4):
                                h = h0 + j
                                lastv = VEC.scalar_tensor_tensor(
                                    out=rhsg[:, h * 129:h * 129 + 128], in0=ttmp2[:, j * 128:(j + 1) * 128],
                                    scalar=small[:, C_RINV + kv:C_RINV + kv + 1],
                                    in1=mb_sb[:, ti * D + h * 128: ti * D + (h + 1) * 128],
                                    op0=ALU.mult, op1=ALU.mult)
                            p.inc(lastv, "ve")

                        for kv in range(HKV):
                            VEC.tensor_scalar_mul(small[:, C_COEF + kv * 4:C_COEF + (kv + 1) * 4],
                                                  small[:, C_DENP + kv * 4:C_DENP + (kv + 1) * 4],
                                                  small[:, C_RMKSQ + kv:C_RMKSQ + kv + 1])
                        VEC.drain()
                        VEC.tensor_scalar(small[:, C_COEF:C_COEF + H], small[:, C_COEF:C_COEF + H],
                                          -1.0, 1.0, op0=ALU.mult, op1=ALU.add)
                        VEC.drain()
                        VEC.tensor_scalar_max(small[:, C_COEF:C_COEF + H], small[:, C_COEF:C_COEF + H], 0.0)
                        VEC.drain()
                        VEC.tensor_scalar_min(small[:, C_COEF:C_COEF + H], small[:, C_COEF:C_COEF + H], 1.0)
                        VEC.drain()
                        for kv in range(HKV):
                            VEC.tensor_scalar_mul(small[:, C_COEF + kv * 4:C_COEF + (kv + 1) * 4],
                                                  small[:, C_COEF + kv * 4:C_COEF + (kv + 1) * 4],
                                                  small[:, C_RINV + kv:C_RINV + kv + 1])
                        VEC.drain()
                        v = VEC.tensor_copy(rhsg.ap().rearrange("p (h w) -> p h w", w=129)[:, :, 128],
                                            small[:, C_COEF:C_COEF + H])
                        rhs_tok = p.inc(v, "ve")

                        for h in range(H):
                            kv = h // 4
                            for fc in range(FCN):
                                i = h * FCN + fc
                                bank, bn = (psE1, "psE1") if i % 2 == 0 else (psE2, "psE2")
                                p.acquire(TEN, bn)
                                p.wait(TEN, rhs_tok)
                                last = TEN.matmul(bank[:, 0:129],
                                                  mk_sb[:, kv * 768 + fc * 128: kv * 768 + (fc + 1) * 128],
                                                  rhsg[:, h * 129:(h + 1) * 129],
                                                  start=True, stop=True)
                                mt = p.inc(last, "pe")
                                slx = i % 2
                                if dtmp_tok[slx] is not None:
                                    p.wait(VEC, dtmp_tok[slx])
                                p.wait(VEC, mt)
                                ev = VEC.tensor_copy(dtmp[:, slx * 129:(slx + 1) * 129], bank[:, 0:129])
                                et = p.inc(ev, "ve")
                                p.release(bn, et)
                                p.wait(GPS, et)
                                p.wait(GPS, tok_dz)
                                p.wait(GPS, dslice_tok.get(i))
                                d = GPS.dma_start(out=delta_d.ap()[:, i * 129:(i + 1) * 129],
                                                  in_=dtmp[:, slx * 129:(slx + 1) * 129],
                                                  accum_op=ALU.add)
                                dtmp_tok[slx] = p.inc(d, "lg", 16)
                                dslice_tok[i] = dtmp_tok[slx]
                        cend_pe[g] = p.tok("pe")

                    for g in range(NTC):
                        xs = p1_T(g)
                        p1_P(g, xs)
                        if g > 0:
                            p1_C(g - 1)
                    p1_C(NTC - 1)

                    p.full_barrier()
                    c = GPS.collective_compute(
                        "AllReduce", ALU.add,
                        replica_groups=[list(range(NCORES))],
                        ins=[delta_d.ap().opt()],
                        outs=[ccout_d.ap().opt()],
                    )
                    ct = p.inc(c, "cc")
                    p.wait(SYN, ct)
                    CH = 3072
                    for i in range((MEMN_W + CH - 1) // CH):
                        w = min(CH, MEMN_W - i * CH)
                        t = SYN.dma_start(out=mk_sb[:, 0:w] if i % 2 == 0 else mkT_sb[:, 0:w],
                                          in_=ccout_d.ap()[:, i * CH:i * CH + w])
                        lt = p.inc(t, "lw", 16)
                        p.wait(VEC, lt)
                        buf = mk_sb if i % 2 == 0 else mkT_sb
                        v = VEC.tensor_add(memn[:, i * CH:i * CH + w], memn[:, i * CH:i * CH + w],
                                           buf[:, 0:w])
                        vt = p.inc(v, "ve")
                        p.wait(SYN, vt)
                    p.full_barrier()

                with nc.sbuf_tensor("q_sb", [128, 2 * D], F32) as q_sb, \
                     nc.sbuf_tensor("gate_sb", [128, 2 * D], F32) as gate_sb, \
                     nc.sbuf_tensor("aq_sb", [128, 512], F32) as aq_sb, \
                     nc.sbuf_tensor("mq_sb", [128, 2 * 768], F32) as mq_sb, \
                     nc.sbuf_tensor("mqT_sb", [128, 2 * 768], F32) as mqT_sb, \
                     nc.sbuf_tensor("y1T_th", [128, D], BF16) as y1T_th, \
                     nc.sbuf_tensor("y1T_tl", [128, D], BF16) as y1T_tl, \
                     nc.sbuf_tensor("ttmp3", [128, 256], F32) as ttmp:

                    ttmp3b = aq_sb
                    xT_read = {}
                    xtm_read = None
                    xtm_free = None
                    proj_done = {}
                    proj_mm = {}
                    qread = {}
                    cend_pe = {}
                    cdpfp = {}
                    y1T_used = None
                    y1T_store_tok = None

                    def p3_T(g):
                        nonlocal xtm_read
                        sl = g % 2
                        if xtm_read is not None:
                            p.wait(SYN, xtm_read)
                        if xtm_free is not None:
                            for t_ in xtm_free:
                                p.wait(SYN, t_)
                        t = SYN.dma_start(out=x_tm[:], in_=x_d.ap()[g * 128:(g + 1) * 128, :])
                        xt = p.inc(t, "lw", 16)
                        srcs = [x_tm[:, kc * 128:(kc + 1) * 128] for kc in range(KC)]
                        dsth = [xT_h[:, (sl * KC + kc) * 128:(sl * KC + kc + 1) * 128] for kc in range(KC)]
                        dstl = [xT_l[:, (sl * KC + kc) * 128:(sl * KC + kc + 1) * 128] for kc in range(KC)]
                        et = transpose_batch(srcs, dsth, dstl, xt, extra_wait=xT_read.get(sl))
                        xtm_read = p.tok("pe")
                        return et

                    def p3_P(g, xs_tok):
                        sl = g % 2
                        slices = []
                        for ds in range(8):
                            slices.append((cache["wq"][0], cache["wq"][1], ds * 256, BOFF["q"] + ds * 256,
                                           (lambda ti, ds=ds: q_sb[:, ti * D + ds * 256: ti * D + ds * 256 + 256]), None))
                        for ds in range(8):
                            slices.append((cache["wg"][0], cache["wg"][1], ds * 256, BOFF["g"] + ds * 256,
                                           (lambda ti, ds=ds: gate_sb[:, ti * D + ds * 256: ti * D + ds * 256 + 256]),
                                           AF.Sigmoid))
                        qt = qread.get(g - 2)
                        if qt is not None:
                            for t_ in qt:
                                p.wait(VEC, t_)
                                p.wait(ACT, t_)
                        mt = proj_pairs(slices, xs_tok,
                                        (lambda ti, kc: xT_h[:, (ti * KC + kc) * 128:(ti * KC + kc + 1) * 128]),
                                        (lambda ti, kc: xT_l[:, (ti * KC + kc) * 128:(ti * KC + kc + 1) * 128]),
                                        prev_used=proj_mm.get(g - 1), tis=[sl])
                        proj_mm[g] = mt
                        xT_read[sl] = mt
                        proj_done[g] = [p.tok("ve"), p.tok("se")]

                    def p3_C(g):
                        nonlocal xtm_free, y1T_used, y1T_store_tok
                        ti = g % 2
                        tc = g
                        p.wait(SYN, xtm_read)
                        if xtm_free is not None:
                            for t_ in xtm_free:
                                p.wait(SYN, t_)
                        p.wait(SYN, cdpfp.get(g - 1))
                        p.wait(SYN, qread.get(g - 1) and qread[g - 1][1])
                        t = SYN.dma_start(out=x_tm[:], in_=x_d.ap()[tc * 128:(tc + 1) * 128, :])
                        x_tok = p.inc(t, "lw", 16)

                        p.wait_all(ACT, proj_done[g])
                        p.wait_all(VEC, proj_done[g])
                        z_ve = None
                        for hg in range(8):
                            h0 = hg * 2
                            mkt = dpfp(aq_sb[:], mq_sb[:], 2,
                                       q_sb[:, ti * D + h0 * 128: ti * D + (h0 + 2) * 128],
                                       pe_wait=(cend_pe.get(g - 1) if hg == 0 else None),
                                       ve_wait=(cdpfp.get(g - 1) if hg == 0 else z_ve))
                            cdpfp[g] = mkt
                            nt = mk_normalize(mq_sb[:], 2, C_NSQA)
                            srcs = []
                            dsth = []
                            for j in range(2):
                                for fc in range(FCN):
                                    srcs.append(mq_sb[:, j * 768 + fc * 128: j * 768 + (fc + 1) * 128])
                                    dsth.append(mqT_sb[:, (j * FCN + fc) * 128:(j * FCN + fc + 1) * 128])
                            mqT_tok = transpose_batch(srcs, dsth, None, mkt)

                            tiles = [(psE1, 0), (psE1, 130)]
                            p.acquire(TEN, "psE1")
                            p.wait(TEN, mqT_tok)
                            last = None
                            for fc in range(FCN):
                                for j in range(2):
                                    h = h0 + j
                                    mm = TEN.matmul(psE1[:, tiles[j][1]:tiles[j][1] + 129],
                                                    mqT_sb[:, (j * FCN + fc) * 128:(j * FCN + fc + 1) * 128],
                                                    memn[:, (h * FCN + fc) * 129:(h * FCN + fc + 1) * 129],
                                                    start=(fc == 0 and j == 0),
                                                    stop=(fc == FCN - 1), skip_group_check=True)
                                    last = mm
                            mt = p.inc(last, "pe")
                            p.wait(VEC, mt)
                            for j in range(2):
                                VEC.tensor_copy(small[:, C_DENP + h0 + j:C_DENP + h0 + j + 1],
                                                psE1[:, j * 130 + 128:j * 130 + 129])
                            VEC.drain()
                            VEC.tensor_tensor(small[:, C_DENP + h0:C_DENP + h0 + 2],
                                              small[:, C_DENP + h0:C_DENP + h0 + 2],
                                              small[:, C_RINV:C_RINV + 2], op=ALU.mult)
                            VEC.drain()
                            VEC.tensor_scalar_add(small[:, C_DENP + h0:C_DENP + h0 + 2],
                                                  small[:, C_DENP + h0:C_DENP + h0 + 2], EPS_A)
                            VEC.drain()
                            VEC.reciprocal(small[:, C_RD + h0:C_RD + h0 + 2],
                                           small[:, C_DENP + h0:C_DENP + h0 + 2])
                            VEC.drain()
                            v = VEC.tensor_tensor(small[:, C_RDP + h0:C_RDP + h0 + 2],
                                                  small[:, C_RD + h0:C_RD + h0 + 2],
                                                  small[:, C_RINV:C_RINV + 2], op=ALU.mult)
                            vt = p.inc(v, "ve")
                            p.wait(ACT, vt)
                            lasts = None
                            for j in range(2):
                                h = h0 + j
                                lasts = ACT.activation(out=ttmp[:, j * 128:(j + 1) * 128],
                                                       in_=psE1[:, tiles[j][1]:tiles[j][1] + 128],
                                                       func=AF.Identity,
                                                       scale=small[:, C_RDP + h:C_RDP + h + 1])
                            st = p.inc(lasts, "se")
                            p.release("psE1", st)
                            p.wait(VEC, st)
                            p.wait(VEC, x_tok)
                            for j in range(2):
                                h = h0 + j
                                VEC.tensor_mul(ttmp3b[:, j * 128:(j + 1) * 128], ttmp[:, j * 128:(j + 1) * 128],
                                               gate_sb[:, ti * D + h * 128: ti * D + (h + 1) * 128])
                            VEC.drain()
                            lastv = None
                            for j in range(2):
                                h = h0 + j
                                lastv = VEC.scalar_tensor_tensor(
                                    out=x_tm[:, h * 128:(h + 1) * 128],
                                    in0=x_tm[:, h * 128:(h + 1) * 128], scalar=2.0,
                                    in1=ttmp3b[:, j * 128:(j + 1) * 128], op0=ALU.mult, op1=ALU.add)
                            z_ve = p.inc(lastv, "ve")
                        qread[g] = [p.tok("se"), p.tok("ve")]

                        lt = ln_chain(x_tm[:], x_tm[:])
                        p.wait(GPS, lt)
                        p.dma(GPS, "lg", y1_d.ap()[tc * 128:(tc + 1) * 128, :], x_tm[:])
                        if y1T_used is not None:
                            p.wait(VEC, y1T_used)
                        if y1T_store_tok is not None:
                            p.wait(VEC, y1T_store_tok)
                        srcs = [x_tm[:, kc * 128:(kc + 1) * 128] for kc in range(KC)]
                        dsth = [y1T_th[:, kc * 128:(kc + 1) * 128] for kc in range(KC)]
                        dstl = [y1T_tl[:, kc * 128:(kc + 1) * 128] for kc in range(KC)]
                        yt = transpose_batch(srcs, dsth, dstl, lt)
                        p.wait(GPS, yt)
                        p.dma(GPS, "lg", y1T_h_d.ap()[tc], y1T_th[:])
                        y1T_st = p.dma(GPS, "lg", y1T_l_d.ap()[tc], y1T_tl[:])
                        xtm_free = [p.tok("pe"), p.tok("lg")]
                        y1T_store_tok = y1T_st

                        p.acquire(TEN, "psG")
                        p.wait(TEN, yt)
                        last = None
                        for kc in range(KC):
                            yh = y1T_th[:, kc * 128:(kc + 1) * 128]
                            yl = y1T_tl[:, kc * 128:(kc + 1) * 128]
                            gh = gwT_h[:, kc * 8:(kc + 1) * 8]
                            gl = gwT_l[:, kc * 8:(kc + 1) * 8]
                            last = TEN.matmul(psG[0:8, 0:128], gh, yh, start=(kc == 0), stop=False)
                            last = TEN.matmul(psG[0:8, 0:128], gl, yh, start=False, stop=False)
                            last = TEN.matmul(psG[0:8, 0:128], gh, yl, start=False,
                                              stop=(kc == KC - 1))
                        mt = p.inc(last, "pe")
                        p.wait(VEC, mt)
                        v = VEC.tensor_copy(small[0:8, C_LGT:C_LGT + 128], psG[0:8, 0:128])
                        vt = p.inc(v, "ve")
                        p.wait(TEN, vt)
                        tr = TEN.transpose(psG[:, 128:136], small[0:8, C_LGT:C_LGT + 128],
                                           ident[0:8, 0:8])
                        mt = p.inc(tr, "pe")
                        p.wait(VEC, mt)
                        lg = small[:, C_LTM:C_LTM + 8]
                        v = VEC.tensor_copy(lg, psG[:, 128:136])
                        p.release("psG", p.inc(v, "ve"))
                        VEC.drain()
                        g8 = lambda i: small[:, C_G8 + i * 8:C_G8 + (i + 1) * 8]
                        max8, lz1, lz2, msk1, msk2, mr1, mr2 = (g8(i) for i in range(7))
                        VEC.max(out=max8, in_=lg)
                        VEC.memset(mr1, -1e30)
                        VEC.memset(mr2, -1e30)
                        VEC.drain()
                        VEC.tensor_copy(mr1[:, 0:1], max8[:, 0:1])
                        VEC.tensor_copy(mr2[:, 0:1], max8[:, 1:2])
                        VEC.drain()
                        VEC.match_replace(out=lz1, in_to_replace=mr1, in_values=lg, imm_value=-1e38)
                        VEC.drain()
                        VEC.match_replace(out=lz2, in_to_replace=mr2, in_values=lz1, imm_value=-1e38)
                        VEC.drain()
                        VEC.tensor_tensor(msk1, lz1, lg, op=ALU.not_equal)
                        VEC.tensor_tensor(msk2, lz2, lz1, op=ALU.not_equal)
                        v = VEC.tensor_sub(small[:, C_GS:C_GS + 1], max8[:, 0:1], max8[:, 1:2])
                        vt = p.inc(v, "ve")
                        p.wait(ACT, vt)
                        ACT.activation(out=small[:, C_GS + 1:C_GS + 2], in_=small[:, C_GS:C_GS + 1],
                                       func=AF.Sigmoid)
                        s = ACT.activation(out=small[:, C_GS + 2:C_GS + 3], in_=small[:, C_GS:C_GS + 1],
                                           func=AF.Sigmoid, scale=-1.0)
                        st = p.inc(s, "se")
                        p.wait(VEC, st)
                        VEC.tensor_scalar_mul(msk1, msk1, small[:, C_GS + 1:C_GS + 2])
                        VEC.tensor_scalar_mul(msk2, msk2, small[:, C_GS + 2:C_GS + 3])
                        VEC.drain()
                        v = VEC.tensor_add(cmb_all[:, tc * 8:(tc + 1) * 8], msk1, msk2)
                        p.inc(v, "ve")
                        y1T_used = p.tok("pe")
                        cend_pe[g] = p.tok("pe")

                    for g in range(NTC):
                        xs = p3_T(g)
                        p3_P(g, xs)
                        if g > 0:
                            p3_C(g - 1)
                    p3_C(NTC - 1)

                    p.full_barrier()

            FG = 2
            with nc.sbuf_tensor("y1Tg_h", [128, FG * D], BF16) as y1Tg_h, \
                 nc.sbuf_tensor("y1Tg_l", [128, FG * D], BF16) as y1Tg_l, \
                 nc.sbuf_tensor("sg_g", [128, FG * 512], F32) as sg_g, \
                 nc.sbuf_tensor("act_g", [128, FG * II], F32) as act_g, \
                 nc.sbuf_tensor("actT_h", [128, FG * II], BF16) as actT_h, \
                 nc.sbuf_tensor("actT_l", [128, FG * II], BF16) as actT_l, \
                 nc.sbuf_tensor("out_acc", [128, FG * D], F32) as out_acc, \
                 nc.sbuf_tensor("wt3_h", [128, 8192], BF16) as wt3_h, \
                 nc.sbuf_tensor("wt3_l", [128, 8192], BF16) as wt3_l, \
                 nc.sbuf_tensor("wt4_h", [128, 8192], BF16) as wt4_h, \
                 nc.sbuf_tensor("wt4_l", [128, 8192], BF16) as wt4_l, \
                 nc.sbuf_tensor("y1ld", [128, D], F32) as y1ld:

                outb = act_g
                wsets = [((wt1_h, wt1_l), (wt2_h, wt2_l)),
                         ((wt3_h, wt3_l), (wt4_h, wt4_l))]
                final_store = None
                y1g_used = None
                pending_final = None

                def emit_final(gf, otoks):
                    nonlocal final_store
                    for ti in range(FG):
                        tc = gf * FG + ti
                        if final_store is not None:
                            p.wait(SYN, final_store)
                        t = SYN.dma_start(out=y1ld[:], in_=y1_d.ap()[tc * 128:(tc + 1) * 128, :])
                        lt = p.inc(t, "lw", 16)
                        p.wait(VEC, lt)
                        for dsx in range(4):
                            p.wait(VEC, otoks.get((ti, dsx)))
                        v = VEC.tensor_add(y1ld[:], y1ld[:], out_acc[:, ti * D:(ti + 1) * D])
                        vt = p.inc(v, "ve")
                        if final_store is not None:
                            p.wait(ACT, final_store)
                        lnt = ln_chain(y1ld[:], y1ld[:])
                        p.wait(GPS, lnt)
                        t = GPS.dma_start(out=out_d.ap()[tc * 128:(tc + 1) * 128, :], in_=y1ld[:])
                        final_store = p.inc(t, "lg", 16)

                for g in range(NTC // FG):
                    if y1g_used is not None:
                        p.wait(SYN, y1g_used)
                    p.dma(SYN, "lw", y1Tg_h.ap().rearrange("p (t c) -> p t c", t=FG),
                          y1T_h_d.ap()[g * FG:(g + 1) * FG].rearrange("t p c -> p t c"))
                    y1g_tok = p.dma(SYN, "lw", y1Tg_l.ap().rearrange("p (t c) -> p t c", t=FG),
                                    y1T_l_d.ap()[g * FG:(g + 1) * FG].rearrange("t p c -> p t c"))

                    stages = []
                    for e in range(E):
                        for (c0, w) in W1_SLICES:
                            stages.append(("w1", e, c0, w))
                        for dsp in range(2):
                            stages.append(("w2", e, dsp))
                    load_toks = {}
                    mm_toks = {}

                    def emit_ffn_load(si):
                        kind, e, a1, a2 = stages[si][0], stages[si][1], stages[si][2], stages[si][3] if len(stages[si]) > 3 else None
                        (b1h, b1l), (b2h, b2l) = wsets[si % 2]
                        if si >= 2 and (si - 2) in mm_toks:
                            p.wait(SYN, mm_toks[si - 2])
                        if kind == "w1":
                            c0, w = a1, a2
                            last = None
                            for kc in range(KC):
                                p.dma(SYN, "lw", b1h[:, kc * 512:kc * 512 + w],
                                      cache["w1"][0].ap()[e, kc * 128:(kc + 1) * 128, c0:c0 + w])
                                p.dma(SYN, "lw", b1l[:, kc * 512:kc * 512 + w],
                                      cache["w1"][1].ap()[e, kc * 128:(kc + 1) * 128, c0:c0 + w])
                                p.dma(SYN, "lw", b2h[:, kc * 512:kc * 512 + w],
                                      cache["w1"][0].ap()[e, kc * 128:(kc + 1) * 128, 1408 + c0:1408 + c0 + w])
                                last = p.dma(SYN, "lw", b2l[:, kc * 512:kc * 512 + w],
                                             cache["w1"][1].ap()[e, kc * 128:(kc + 1) * 128, 1408 + c0:1408 + c0 + w])
                        else:
                            dsp = a1
                            ds0, ds1 = 2 * dsp, 2 * dsp + 1
                            last = None
                            for ic in range(ICN):
                                p.dma(SYN, "lw", b1h[:, ic * 512:(ic + 1) * 512],
                                      cache["w2"][0].ap()[e, ic * 128:(ic + 1) * 128, ds0 * 512:(ds0 + 1) * 512])
                                p.dma(SYN, "lw", b1l[:, ic * 512:(ic + 1) * 512],
                                      cache["w2"][1].ap()[e, ic * 128:(ic + 1) * 128, ds0 * 512:(ds0 + 1) * 512])
                                p.dma(SYN, "lw", b2h[:, ic * 512:(ic + 1) * 512],
                                      cache["w2"][0].ap()[e, ic * 128:(ic + 1) * 128, ds1 * 512:(ds1 + 1) * 512])
                                last = p.dma(SYN, "lw", b2l[:, ic * 512:(ic + 1) * 512],
                                             cache["w2"][1].ap()[e, ic * 128:(ic + 1) * 128, ds1 * 512:(ds1 + 1) * 512])
                        load_toks[si] = last

                    oacc_toks = {}
                    act_tok = None
                    aT_tok = None
                    w2mm_tok = None
                    sg_read = {}
                    emit_ffn_load(0)
                    for si, st_ in enumerate(stages):
                        if si + 1 < len(stages):
                            emit_ffn_load(si + 1)
                        kind, e = st_[0], st_[1]
                        (b1h, b1l), (b2h, b2l) = wsets[si % 2]
                        lt = load_toks[si]
                        if kind == "w1":
                            c0, w = st_[2], st_[3]
                            for ti in range(FG):
                                tc = g * FG + ti
                                bks = [(psA, "psA"), (psB, "psB")] if ti % 2 == 0 else [(psE1, "psE1"), (psE2, "psE2")]
                                mt = mm3m(
                                    (lambda kc, ti=ti: y1Tg_h[:, (ti * KC + kc) * 128:(ti * KC + kc + 1) * 128]),
                                    (lambda kc, ti=ti: y1Tg_l[:, (ti * KC + kc) * 128:(ti * KC + kc + 1) * 128]),
                                    [(bks[0][0], bks[0][1],
                                      (lambda kc: b1h[:, kc * 512:kc * 512 + w]),
                                      (lambda kc: b1l[:, kc * 512:kc * 512 + w]), w, None),
                                     (bks[1][0], bks[1][1],
                                      (lambda kc: b2h[:, kc * 512:kc * 512 + w]),
                                      (lambda kc: b2l[:, kc * 512:kc * 512 + w]), w, None)],
                                    KC, waits=[y1g_tok, lt])
                                mm_toks[si] = mt
                                p.wait(ACT, mt)
                                if sg_read.get(ti) is not None:
                                    p.wait(ACT, sg_read[ti])
                                s = ACT.activation(out=sg_g[:, ti * 512:ti * 512 + w],
                                                   in_=bks[0][0][:, 0:w], func=AF.Silu)
                                stk = p.inc(s, "se")
                                p.release(bks[0][1], stk)
                                p.wait(VEC, stk)
                                v = VEC.scalar_tensor_tensor(
                                    out=act_g[:, ti * II + c0: ti * II + c0 + w],
                                    in0=bks[1][0][:, 0:w], scalar=cmb_all[:, tc * 8 + e:tc * 8 + e + 1],
                                    in1=sg_g[:, ti * 512:ti * 512 + w],
                                    op0=ALU.mult, op1=ALU.mult)
                                vt = p.inc(v, "ve")
                                p.release(bks[1][1], vt)
                                sg_read[ti] = vt
                                act_tok = vt
                            if st_[2] == W1_SLICES[-1][0]:
                                for ti in range(FG):
                                    if w2mm_tok is not None:
                                        p.wait(VEC, w2mm_tok)
                                    srcs = [act_g[:, ti * II + ic * 128: ti * II + (ic + 1) * 128] for ic in range(ICN)]
                                    dsth = [actT_h[:, (ti * ICN + ic) * 128:(ti * ICN + ic + 1) * 128] for ic in range(ICN)]
                                    dstl = [actT_l[:, (ti * ICN + ic) * 128:(ti * ICN + ic + 1) * 128] for ic in range(ICN)]
                                    aT_tok = transpose_batch(srcs, dsth, dstl, act_tok)
                        else:
                            dsp = st_[2]
                            ds0, ds1 = 2 * dsp, 2 * dsp + 1
                            for ti in range(FG):
                                bks = [(psA, "psA"), (psB, "psB")] if ti % 2 == 0 else [(psE1, "psE1"), (psE2, "psE2")]
                                mt = mm3m(
                                    (lambda ic, ti=ti: actT_h[:, (ti * ICN + ic) * 128:(ti * ICN + ic + 1) * 128]),
                                    (lambda ic, ti=ti: actT_l[:, (ti * ICN + ic) * 128:(ti * ICN + ic + 1) * 128]),
                                    [(bks[0][0], bks[0][1],
                                      (lambda ic: b1h[:, ic * 512:(ic + 1) * 512]),
                                      (lambda ic: b1l[:, ic * 512:(ic + 1) * 512]), 512, None),
                                     (bks[1][0], bks[1][1],
                                      (lambda ic: b2h[:, ic * 512:(ic + 1) * 512]),
                                      (lambda ic: b2l[:, ic * 512:(ic + 1) * 512]), 512, None)],
                                    ICN, waits=[aT_tok, lt])
                                mm_toks[si] = mt
                                w2mm_tok = mt
                                for i, ds in enumerate((ds0, ds1)):
                                    bank, bn = bks[i]
                                    p.wait(VEC, mt)
                                    dsl = out_acc[:, ti * D + ds * 512: ti * D + (ds + 1) * 512]
                                    if e == 0:
                                        ev = VEC.tensor_copy(dsl, bank[:, 0:512])
                                    else:
                                        k = oacc_toks.get((ti, ds))
                                        if k is not None:
                                            p.wait(VEC, k)
                                        ev = VEC.tensor_add(dsl, dsl, bank[:, 0:512])
                                    et = p.inc(ev, "ve")
                                    oacc_toks[(ti, ds)] = et
                                    p.release(bn, et)

                    emit_final(g, dict(oacc_toks))
                    y1g_used = p.tok("pe")

                p.wait(GPS, final_store)

    es_sem.close()
    es_ps.close()
    es_perm.close()
    return nc


_NC_CACHE = None


def kernel(**inputs):
    global _NC_CACHE
    x = np.asarray(inputs["L0_attn_hidden_states"], dtype=np.float32).reshape(TOK * NCORES, D)
    Wq = np.asarray(inputs["L0_attn_Wq"], dtype=np.float32)
    bq = np.asarray(inputs["L0_attn_bq"], dtype=np.float32)
    Wk = np.asarray(inputs["L0_attn_Wk"], dtype=np.float32)
    bk = np.asarray(inputs["L0_attn_bk"], dtype=np.float32)
    Wv = np.asarray(inputs["L0_attn_Wv"], dtype=np.float32)
    bv = np.asarray(inputs["L0_attn_bv"], dtype=np.float32)
    Wg = np.asarray(inputs["L0_attn_Wg"], dtype=np.float32)
    bg = np.asarray(inputs["L0_attn_bg"], dtype=np.float32)
    Wmb = np.asarray(inputs["L0_attn_Wmb"], dtype=np.float32)
    bmb = np.asarray(inputs["L0_attn_bmb"], dtype=np.float32)
    memory = np.asarray(inputs["L0_attn_memory"], dtype=np.float32)
    norm = np.asarray(inputs["L0_attn_norm"], dtype=np.float32)
    gate_w = np.asarray(inputs["L0_ffn_gate_w"], dtype=np.float32)
    w1 = np.asarray(inputs["L0_ffn_w1"], dtype=np.float32)
    w2 = np.asarray(inputs["L0_ffn_w2"], dtype=np.float32)

    WqT = np.ascontiguousarray(Wq.T)
    WkT = np.ascontiguousarray(Wk.T)
    WvT = np.ascontiguousarray(Wv.T)
    WgT = np.ascontiguousarray(Wg.T)
    WmbT = np.ascontiguousarray(Wmb.T)
    gwT = np.ascontiguousarray(gate_w.T)
    w1T = np.ascontiguousarray(np.transpose(w1, (0, 2, 1)))
    w2T = np.ascontiguousarray(np.transpose(w2, (0, 2, 1)))
    mem_r = memory.reshape(H, FCN, 128, DH).transpose(2, 0, 1, 3)
    nrm_r = norm.reshape(H, FCN, 128).transpose(2, 0, 1)[..., None]
    memn = np.ascontiguousarray(
        np.concatenate([mem_r, nrm_r], axis=-1).reshape(128, MEMN_W).astype(np.float32))
    bpack = np.concatenate([bq, bk, bv, bg, bmb]).reshape(1, 7168).astype(np.float32)
    ident = np.eye(128, dtype=np.float32)

    if _NC_CACHE is None:
        _NC_CACHE = build_core()
    nc = _NC_CACHE

    shared = {"WqT": WqT, "WkT": WkT, "WvT": WvT, "WgT": WgT, "WmbT": WmbT,
              "gwT": gwT, "w1T": w1T, "w2T": w2T, "memn": memn,
              "bpack": bpack, "ident": ident}
    in_maps = [dict(shared, x=np.ascontiguousarray(x[c * TOK:(c + 1) * TOK]))
               for c in range(NCORES)]
    res = run_bass_kernel_spmd(nc, in_maps, core_ids=list(range(NCORES)))
    out = np.concatenate([r["out"] for r in res.results], axis=0)
    return out.reshape(2, 8192, 2048).astype(np.float32)


if __name__ == "__main__":
    pass


# version 14 (final)
# speedup vs baseline: 1.0251x; 1.0158x over previous
import copy as _copy
import numpy as np
from contextlib import ExitStack

import concourse.bass as bass
import concourse.mybir as mybir
from concourse.bass_utils import run_bass_kernel_spmd

F32 = mybir.dt.float32
BF16 = mybir.dt.bfloat16
AF = mybir.ActivationFunctionType
ALU = mybir.AluOpType

NCORES = 8
TOK = 2048
NTC = 16
TCG = 2
NG = NTC // TCG
D = 2048
KC = 16
H, HKV, DH = 16, 4, 128
FD, FCN = 768, 6
E, II, GUD = 8, 1408, 2816
ICN = 11
EPS_A = 1e-8
EPS_LN = 1e-5

W1_SLICES = [(0, 512), (512, 512), (1024, 384)]
MEMN_W = H * FCN * 129


class P:

    def __init__(self, nc, es):
        self.nc = nc
        names = ["lw", "lg", "pe", "ve", "se", "cc"]
        self.sem = {n: es.enter_context(nc.semaphore(n)) for n in names}
        self.cnt = {n: 0 for n in names}
        self.bank_free = {}

    def inc(self, instr, s, k=1):
        self.cnt[s] += k
        instr.then_inc(self.sem[s], k)
        return (s, self.cnt[s])

    def tok(self, s):
        return (s, self.cnt[s])

    def wait(self, eng, t):
        if t is not None and t[1] > 0:
            eng.wait_ge(self.sem[t[0]], t[1])

    def wait_all(self, eng, toks):
        for t in toks:
            self.wait(eng, t)

    def acquire(self, eng, bank):
        t = self.bank_free.get(bank)
        if t is not None:
            self.wait(eng, t)
            self.bank_free[bank] = None

    def release(self, bank, t):
        self.bank_free[bank] = t

    def full_barrier(self):
        engs = [self.nc.tensor, self.nc.vector, self.nc.scalar,
                self.nc.gpsimd, self.nc.sync]
        for e in engs:
            for s in ("lw", "lg", "pe", "ve", "se", "cc"):
                if self.cnt[s] > 0:
                    e.wait_ge(self.sem[s], self.cnt[s])

    def guard(self):
        nc = self.nc
        for s in ("pe", "se"):
            if self.cnt[s] > 0:
                nc.vector.wait_ge(self.sem[s], self.cnt[s])
        for s in ("pe", "ve"):
            if self.cnt[s] > 0:
                nc.scalar.wait_ge(self.sem[s], self.cnt[s])
        for s in ("ve", "se"):
            if self.cnt[s] > 0:
                nc.tensor.wait_ge(self.sem[s], self.cnt[s])

    def guard_syn(self):
        for s in ("pe", "ve", "se", "lg"):
            if self.cnt[s] > 0:
                self.nc.sync.wait_ge(self.sem[s], self.cnt[s])

    def dma(self, eng, s, out, in_, **kw):
        t = eng.dma_start(out=out, in_=in_, **kw)
        return self.inc(t, s, 16)


def build_core():
    nc = bass.Bass("TRN2", target_bir_lowering=False)

    dt = lambda n, s, k=None: (nc.dram_tensor(n, s, F32, kind=k) if k else
                               nc.dram_tensor(n, s, F32))
    x_d = nc.dram_tensor("x", [TOK, D], F32, kind="ExternalInput")
    wqT_d = nc.dram_tensor("WqT", [D, D], F32, kind="ExternalInput")
    wkT_d = nc.dram_tensor("WkT", [D, 512], F32, kind="ExternalInput")
    wvT_d = nc.dram_tensor("WvT", [D, 512], F32, kind="ExternalInput")
    wgT_d = nc.dram_tensor("WgT", [D, D], F32, kind="ExternalInput")
    wmbT_d = nc.dram_tensor("WmbT", [D, D], F32, kind="ExternalInput")
    gwT_d = nc.dram_tensor("gwT", [D, 8], F32, kind="ExternalInput")
    w1T_d = nc.dram_tensor("w1T", [E, D, GUD], F32, kind="ExternalInput")
    w2T_d = nc.dram_tensor("w2T", [E, II, D], F32, kind="ExternalInput")
    memn_d = nc.dram_tensor("memn", [128, MEMN_W], F32, kind="ExternalInput")
    bias_d = nc.dram_tensor("bpack", [1, 7168], F32, kind="ExternalInput")
    ident_d = nc.dram_tensor("ident", [128, 128], F32, kind="ExternalInput")
    out_d = nc.dram_tensor("out", [TOK, D], F32, kind="ExternalOutput")

    cache = {}
    for name, shape in [("wq", [D, D]), ("wk", [D, 512]), ("wv", [D, 512]),
                        ("wg", [D, D]), ("wmb", [D, D]),
                        ("w1", [E, D, GUD]), ("w2", [E, II, D])]:
        cache[name] = (nc.dram_tensor(f"c_{name}_h", shape, BF16),
                       nc.dram_tensor(f"c_{name}_l", shape, BF16))
    delta_d = nc.dram_tensor("delta", [128, MEMN_W], F32)
    ccout_d = nc.dram_tensor("ccout", [128, MEMN_W], F32)
    y1_d = nc.dram_tensor("y1", [TOK, D], F32)
    y1T_h_d = nc.dram_tensor("y1T_h", [NTC, 128, D], BF16)
    y1T_l_d = nc.dram_tensor("y1T_l", [NTC, 128, D], BF16)

    BOFF = {"q": 0, "k": 2048, "v": 2560, "g": 3072, "mb": 5120}

    es_perm = ExitStack()
    ident = es_perm.enter_context(nc.sbuf_tensor("ident_sb", [128, 128], F32))
    ones = es_perm.enter_context(nc.sbuf_tensor("ones_sb", [128, 128], F32))
    bias_sb = es_perm.enter_context(nc.sbuf_tensor("bias_sb", [1, 1024], F32))
    small = es_perm.enter_context(nc.sbuf_tensor("small_sb", [128, 448], F32))
    gwT_h = es_perm.enter_context(nc.sbuf_tensor("gwT_h", [128, 128], BF16))
    gwT_l = es_perm.enter_context(nc.sbuf_tensor("gwT_l", [128, 128], BF16))
    cmb_all = es_perm.enter_context(nc.sbuf_tensor("cmb_all", [128, 128], F32))
    wt1_h = es_perm.enter_context(nc.sbuf_tensor("wt1_h", [128, 8192], BF16))
    wt1_l = es_perm.enter_context(nc.sbuf_tensor("wt1_l", [128, 8192], BF16))
    wt2_h = es_perm.enter_context(nc.sbuf_tensor("wt2_h", [128, 8192], BF16))
    wt2_l = es_perm.enter_context(nc.sbuf_tensor("wt2_l", [128, 8192], BF16))

    C_NSQA, C_NSQB, C_S, C_RINV, C_MKSQ, C_RMKSQ = 0, 4, 8, 12, 16, 20
    C_DENP, C_COEF, C_RD = 24, 40, 56
    C_RDP = 320
    C_LN = 72
    C_GS = 88
    C_EPS = 96
    C_G8 = 104
    C_LGT = 160
    C_LTM = 288

    es_ps = ExitStack()
    psA = es_ps.enter_context(nc.psum_tensor("psA", [128, 512], F32))
    psB = es_ps.enter_context(nc.psum_tensor("psB", [128, 512], F32))
    psT1 = es_ps.enter_context(nc.psum_tensor("psT1", [128, 512], F32))
    psT2 = es_ps.enter_context(nc.psum_tensor("psT2", [128, 512], F32))
    psE1 = es_ps.enter_context(nc.psum_tensor("psE1", [128, 512], F32))
    psE2 = es_ps.enter_context(nc.psum_tensor("psE2", [128, 512], F32))
    psG = es_ps.enter_context(nc.psum_tensor("psG", [128, 512], F32))
    psX = es_ps.enter_context(nc.psum_tensor("psX", [128, 512], F32))

    es_sem = ExitStack()

    with nc.Block() as block:

        @block.sync
        def _(sync):
            p = P(nc, es_sem)
            TEN, VEC, ACT, GPS, SYN = nc.tensor, nc.vector, nc.scalar, nc.gpsimd, nc.sync

            def transpose_batch(srcs, dsts_h, dsts_l, src_tok, extra_wait=None):
                ev_tok = None
                for i0 in range(0, len(srcs), 4):
                    batch = srcs[i0:i0 + 4]
                    bank = psT1 if (i0 // 4) % 2 == 0 else psT2
                    bname = "psT1" if bank is psT1 else "psT2"
                    p.wait(TEN, src_tok)
                    if extra_wait is not None:
                        p.wait(TEN, extra_wait)
                    p.acquire(TEN, bname)
                    last = None
                    for j, s in enumerate(batch):
                        last = TEN.transpose(bank[:, j * 128:(j + 1) * 128], s, ident[:])
                    ptok = p.inc(last, "pe")
                    p.wait(VEC, ptok)
                    lastv = None
                    for j in range(len(batch)):
                        lastv = VEC.tensor_copy(dsts_h[i0 + j], bank[:, j * 128:(j + 1) * 128])
                    if dsts_l is not None:
                        for j in range(len(batch)):
                            lastv = VEC.tensor_sub(dsts_l[i0 + j], bank[:, j * 128:(j + 1) * 128], dsts_h[i0 + j])
                    ev_tok = p.inc(lastv, "ve")
                    p.release(bname, ev_tok)
                return ev_tok

            def load_wT_slice(hi_d, lo_d, k0, nkc, c0, w, dst_h, dst_l, dst_c0, stride):
                t = None
                for g in range(0, nkc, 4):
                    n = min(4, nkc - g)
                    src_h = hi_d.ap()[k0 + g * 128:k0 + (g + n) * 128, c0:c0 + w].rearrange(
                        "(kc p) w -> p kc w", p=128)
                    src_l = lo_d.ap()[k0 + g * 128:k0 + (g + n) * 128, c0:c0 + w].rearrange(
                        "(kc p) w -> p kc w", p=128)
                    dh = dst_h.ap()[:, dst_c0 + g * stride: dst_c0 + (g + n) * stride].rearrange(
                        "p (kc w) -> p kc w", kc=n)
                    dl = dst_l.ap()[:, dst_c0 + g * stride: dst_c0 + (g + n) * stride].rearrange(
                        "p (kc w) -> p kc w", kc=n)
                    p.dma(SYN, "lw", dh, src_h)
                    t = p.dma(SYN, "lw", dl, src_l)
                return t

            def mm3(bank, bank_name, xh_tiles, xl_tiles, wh_ap_fn, wl_ap_fn, w, nkc,
                    seed=None, waits=()):
                p.acquire(TEN, bank_name)
                for t in waits:
                    p.wait(TEN, t)
                first = True
                last = None
                if seed is not None:
                    last = TEN.matmul(bank[:, 0:w], ones[0:1, 0:128], seed,
                                      start=True, stop=False)
                    first = False
                for kc in range(nkc):
                    l = kc == nkc - 1
                    last = TEN.matmul(bank[:, 0:w], xh_tiles(kc), wh_ap_fn(kc),
                                      start=first, stop=False)
                    first = False
                    last = TEN.matmul(bank[:, 0:w], xh_tiles(kc), wl_ap_fn(kc),
                                      start=False, stop=False)
                    last = TEN.matmul(bank[:, 0:w], xl_tiles(kc), wh_ap_fn(kc),
                                      start=False, stop=l)
                return p.inc(last, "pe")

            def noldw(m):
                inst = _copy.replace(m.ins, ldweights=False)
                nc.register_instruction(inst, overwrite=True)

            def mm3m(xh_fn, xl_fn, movings, nkc, waits=()):
                for (bank, bn, _, _, _, _) in movings:
                    p.acquire(TEN, bn)
                for t in waits:
                    p.wait(TEN, t)
                started = [False] * len(movings)
                last = None
                for i, (bank, bn, mvh, mvl, w, seed) in enumerate(movings):
                    if seed is not None:
                        last = TEN.matmul(bank[:, 0:w], ones[0:1, 0:128], seed,
                                          start=True, stop=False)
                        started[i] = True
                pend = []
                for kc in range(nkc):
                    l = kc == nkc - 1
                    first_in_run = True
                    for i, (bank, bn, mvh, mvl, w, seed) in enumerate(movings):
                        last = TEN.matmul(bank[:, 0:w], xh_fn(kc), mvh(kc),
                                          start=not started[i], stop=False)
                        started[i] = True
                        if not first_in_run:
                            pend.append(last)
                        first_in_run = False
                        last = TEN.matmul(bank[:, 0:w], xh_fn(kc), mvl(kc),
                                          start=False, stop=False)
                        pend.append(last)
                    first_in_run = True
                    for i, (bank, bn, mvh, mvl, w, seed) in enumerate(movings):
                        last = TEN.matmul(bank[:, 0:w], xl_fn(kc), mvh(kc),
                                          start=False, stop=l)
                        if not first_in_run:
                            pend.append(last)
                        first_in_run = False
                tok = p.inc(last, "pe")
                for m in pend:
                    noldw(m)
                return tok

            def proj_pairs(slices, xs_tok, xh_of, xl_of, prev_used=None, tis=None):
                W = 256
                SL = KC * W
                if tis is None:
                    tis = list(range(TCG))
                sets = [(wt1_h, wt1_l, 0), (wt2_h, wt2_l, 512)]
                ngrp = (len(slices) + 1) // 2
                load_toks = {}
                mm_toks = {}

                def emit_load(gi):
                    hi_b, lo_b, bcol = sets[gi % 2]
                    if gi >= 2 and (gi - 2) in mm_toks:
                        p.wait(SYN, mm_toks[gi - 2])
                    elif gi < 2 and prev_used is not None:
                        p.wait(SYN, prev_used)
                    toks = []
                    for j, (hi_d, lo_d, c0, boff, dstf, act) in enumerate(slices[gi * 2: gi * 2 + 2]):
                        lt = load_wT_slice(hi_d, lo_d, 0, KC, c0, W, hi_b, lo_b, j * SL, W)
                        bt = p.dma(SYN, "lw", bias_sb[0:1, bcol + j * W: bcol + (j + 1) * W],
                                   bias_d.ap()[0:1, boff:boff + W])
                        toks += [lt, bt]
                    load_toks[gi] = toks

                emit_load(0)
                for gi in range(ngrp):
                    if gi + 1 < ngrp:
                        emit_load(gi + 1)
                    hi_b, lo_b, bcol = sets[gi % 2]
                    grp = slices[gi * 2: gi * 2 + 2]
                    for ti in tis:
                        bks = [(psA, "psA"), (psB, "psB")]
                        movings = []
                        for j in range(len(grp)):
                            movings.append((bks[j][0], bks[j][1],
                                            (lambda kc, j=j: hi_b[:, j * SL + kc * W: j * SL + (kc + 1) * W]),
                                            (lambda kc, j=j: lo_b[:, j * SL + kc * W: j * SL + (kc + 1) * W]),
                                            W, bias_sb[0:1, bcol + j * W: bcol + (j + 1) * W]))
                        mt = mm3m((lambda kc, ti=ti: xh_of(ti, kc)),
                                  (lambda kc, ti=ti: xl_of(ti, kc)),
                                  movings, KC, waits=[xs_tok] + load_toks[gi])
                        mm_toks[gi] = mt
                        for j, (hi_d, lo_d, c0, boff, dstf, act) in enumerate(grp):
                            bank, bn = bks[j]
                            dsl = dstf(ti)
                            if act is None:
                                p.wait(VEC, mt)
                                ev = VEC.tensor_copy(dsl, bank[:, 0:W])
                                et = p.inc(ev, "ve")
                            else:
                                p.wait(ACT, mt)
                                ev = ACT.activation(out=dsl, in_=bank[:, 0:W], func=act)
                                et = p.inc(ev, "se")
                            p.release(bn, et)
                return mm_toks[ngrp - 1]

            def ln_chain(z_ap, out_ap):
                c = C_LN
                zv = p.tok("ve")
                p.wait(ACT, zv)
                VEC.drain()
                t1 = VEC.reduce_sum(small[:, c:c + 1], z_ap, axis=mybir.AxisListType.X)
                vt = p.inc(t1, "ve")
                p.acquire(ACT, "psX")
                lasts = None
                for i in range(4):
                    lasts = ACT.activation(out=psX[:, 0:512], in_=z_ap[:, i * 512:(i + 1) * 512],
                                           func=AF.Square, accum_out=small[:, c + 1 + i:c + 2 + i])
                st = p.inc(lasts, "se")
                p.release("psX", st)
                p.wait(VEC, st)
                v = VEC.reduce_sum(small[:, c + 5:c + 6], small[:, c + 1:c + 5], axis=mybir.AxisListType.X)
                VEC.drain()
                VEC.tensor_scalar_mul(small[:, c + 6:c + 7], small[:, c:c + 1], 1.0 / D)
                VEC.tensor_scalar_mul(small[:, c + 7:c + 8], small[:, c + 5:c + 6], 1.0 / D)
                VEC.drain()
                v = VEC.tensor_mul(small[:, c + 8:c + 9], small[:, c + 6:c + 7], small[:, c + 6:c + 7])
                VEC.drain()
                v = VEC.tensor_sub(small[:, c + 8:c + 9], small[:, c + 7:c + 8], small[:, c + 8:c + 9])
                vt = p.inc(v, "ve")
                p.wait(ACT, vt)
                s = ACT.activation(out=small[:, c + 9:c + 10], in_=small[:, c + 8:c + 9],
                                   func=AF.Sqrt, bias=small[:, C_EPS:C_EPS + 1])
                st = p.inc(s, "se")
                p.wait(VEC, st)
                v = VEC.reciprocal(small[:, c + 9:c + 10], small[:, c + 9:c + 10])
                VEC.drain()
                v = VEC.scalar_tensor_tensor(out=small[:, c + 10:c + 11], in0=small[:, c + 6:c + 7],
                                             scalar=-1.0, in1=small[:, c + 9:c + 10],
                                             op0=ALU.mult, op1=ALU.mult)
                vt = p.inc(v, "ve")
                p.wait(ACT, vt)
                s = ACT.activation(out=out_ap, in_=z_ap, func=AF.Identity,
                                   scale=small[:, c + 9:c + 10], bias=small[:, c + 10:c + 11])
                return p.inc(s, "se")

            def dpfp(a_ap, mkr_ap, nheads, k_src_ap, pe_wait=None, ve_wait=None):
                if pe_wait is not None:
                    p.wait(VEC, pe_wait)
                    p.wait(ACT, pe_wait)
                if ve_wait is not None:
                    p.wait(ACT, ve_wait)
                lasts = None
                for j in range(nheads):
                    lasts = ACT.activation(out=a_ap[:, j * 256:j * 256 + 128],
                                           in_=k_src_ap[:, j * 128:(j + 1) * 128], func=AF.Relu)
                    lasts = ACT.activation(out=a_ap[:, j * 256 + 128:j * 256 + 256],
                                           in_=k_src_ap[:, j * 128:(j + 1) * 128], func=AF.Relu,
                                           scale=-1.0)
                at = p.inc(lasts, "se")
                p.wait(VEC, at)
                lastv = None
                for j in range(nheads):
                    ab = a_ap[:, j * 256:(j + 1) * 256]
                    ob = mkr_ap[:, j * 768:(j + 1) * 768]
                    for r in (1, 2, 3):
                        o0 = (r - 1) * 256
                        lastv = VEC.tensor_mul(ob[:, o0:o0 + r], ab[:, 0:r], ab[:, 256 - r:256])
                        lastv = VEC.tensor_mul(ob[:, o0 + r:o0 + 256], ab[:, r:256], ab[:, 0:256 - r])
                return p.inc(lastv, "ve")

            def mk_normalize(mkr_ap, nheads, c_nsq_base):
                p.wait(ACT, p.tok("ve"))
                p.acquire(ACT, "psX")
                lasts = None
                for j in range(nheads):
                    b = mkr_ap[:, j * 768:(j + 1) * 768]
                    lasts = ACT.activation(out=psX[:, 0:512], in_=b[:, 0:512], func=AF.Square,
                                           accum_out=small[:, C_NSQA + j:C_NSQA + j + 1])
                    lasts = ACT.activation(out=psX[:, 0:256], in_=b[:, 512:768], func=AF.Square,
                                           accum_out=small[:, C_NSQB + j:C_NSQB + j + 1])
                st = p.inc(lasts, "se")
                p.release("psX", st)
                p.wait(VEC, st)
                v = VEC.tensor_add(small[:, C_NSQA:C_NSQA + nheads],
                                   small[:, C_NSQA:C_NSQA + nheads],
                                   small[:, C_NSQB:C_NSQB + nheads])
                vt = p.inc(v, "ve")
                p.wait(ACT, vt)
                s = ACT.activation(out=small[:, C_S:C_S + nheads], in_=small[:, C_NSQA:C_NSQA + nheads],
                                   func=AF.Sqrt)
                st = p.inc(s, "se")
                p.wait(VEC, st)
                VEC.tensor_scalar_max(small[:, C_S:C_S + nheads], small[:, C_S:C_S + nheads], 1e-12)
                VEC.drain()
                VEC.reciprocal(small[:, C_RINV:C_RINV + nheads], small[:, C_S:C_S + nheads])
                VEC.drain()
                VEC.tensor_mul(small[:, C_MKSQ:C_MKSQ + nheads], small[:, C_NSQA:C_NSQA + nheads],
                               small[:, C_RINV:C_RINV + nheads])
                VEC.drain()
                VEC.tensor_mul(small[:, C_MKSQ:C_MKSQ + nheads], small[:, C_MKSQ:C_MKSQ + nheads],
                               small[:, C_RINV:C_RINV + nheads])
                VEC.drain()
                VEC.tensor_scalar_max(small[:, C_RMKSQ:C_RMKSQ + nheads],
                                      small[:, C_MKSQ:C_MKSQ + nheads], 1e-30)
                VEC.drain()
                v = VEC.reciprocal(small[:, C_RMKSQ:C_RMKSQ + nheads],
                                   small[:, C_RMKSQ:C_RMKSQ + nheads])
                return p.inc(v, "ve")

            t = SYN.dma_start(out=ident[:], in_=ident_d.ap())
            tok_id = p.inc(t, "lw", 16)
            VEC.memset(ones[:], 1.0)
            v = VEC.memset(small[:, C_EPS:C_EPS + 1], EPS_LN)
            tok_ones = p.inc(v, "ve")

            with nc.sbuf_tensor("memn_sb", [128, MEMN_W], F32) as memn, \
                 nc.sbuf_tensor("x_tm", [128, D], F32) as x_tm, \
                 nc.sbuf_tensor("xT_h", [128, TCG * D], BF16) as xT_h, \
                 nc.sbuf_tensor("xT_l", [128, TCG * D], BF16) as xT_l:

                t = SYN.dma_start(out=memn[:], in_=memn_d.ap())
                tok_memn = p.inc(t, "lw", 16)

                with nc.sbuf_tensor("gwf", [128, 128], F32) as gwf:
                    for kc in range(KC):
                        t = p.dma(SYN, "lw", gwf[:, kc * 8:(kc + 1) * 8],
                                  gwT_d.ap()[kc * 128:(kc + 1) * 128, :])
                    p.wait(VEC, t)
                    VEC.tensor_copy(gwT_h[:], gwf[:])
                    VEC.drain()
                    v = VEC.tensor_sub(gwT_l[:], gwf[:], gwT_h[:])
                    p.inc(v, "ve")

                    v = VEC.memset(gwf[:], 0.0)
                    zt = p.inc(v, "ve")
                    p.wait(GPS, zt)
                    for i in range(0, MEMN_W, 128):
                        w = min(128, MEMN_W - i)
                        tok_dz = p.dma(GPS, "lg", delta_d.ap()[:, i:i + w], gwf[:, 0:w])
                    p.full_barrier()

                with nc.sbuf_tensor("stg0", [128, GUD], F32) as stg0, \
                     nc.sbuf_tensor("stg1", [128, GUD], F32) as stg1, \
                     nc.sbuf_tensor("sph0", [128, GUD], BF16) as sph0, \
                     nc.sbuf_tensor("sph1", [128, GUD], BF16) as sph1, \
                     nc.sbuf_tensor("spl0", [128, GUD], BF16) as spl0, \
                     nc.sbuf_tensor("spl1", [128, GUD], BF16) as spl1:
                    stg = [stg0, stg1]
                    sph = [sph0, sph1]
                    spl = [spl0, spl1]
                    band_split_tok = {}
                    band_store_tok = {}
                    bands = []
                    for nm, src, ktot, dout in [
                        ("wk", wkT_d, D, 512), ("wv", wvT_d, D, 512),
                        ("wmb", wmbT_d, D, D), ("wq", wqT_d, D, D),
                        ("wg", wgT_d, D, D)]:
                        for b in range(ktot // 128):
                            bands.append((nm, src.ap()[b * 128:(b + 1) * 128, :],
                                          cache[nm][0].ap()[b * 128:(b + 1) * 128, :],
                                          cache[nm][1].ap()[b * 128:(b + 1) * 128, :], dout))
                    for e in range(E):
                        for b in range(KC):
                            bands.append(("w1", w1T_d.ap()[e, b * 128:(b + 1) * 128, :],
                                          cache["w1"][0].ap()[e, b * 128:(b + 1) * 128, :],
                                          cache["w1"][1].ap()[e, b * 128:(b + 1) * 128, :], GUD))
                    for e in range(E):
                        for b in range(ICN):
                            bands.append(("w2", w2T_d.ap()[e, b * 128:(b + 1) * 128, :],
                                          cache["w2"][0].ap()[e, b * 128:(b + 1) * 128, :],
                                          cache["w2"][1].ap()[e, b * 128:(b + 1) * 128, :], D))

                    for bi, (nm, src, dsth, dstl, dout) in enumerate(bands):
                        s = bi % 2
                        if bi >= 2:
                            p.wait(SYN, band_split_tok[bi - 2])
                        t = SYN.dma_start(out=stg[s][:, 0:dout], in_=src)
                        lt = p.inc(t, "lw", 16)
                        p.wait(ACT, lt)
                        if bi >= 2:
                            p.wait(ACT, band_store_tok[bi - 2])
                        sc = ACT.activation(out=sph[s][:, 0:dout], in_=stg[s][:, 0:dout],
                                            func=AF.Copy)
                        ht = p.inc(sc, "se")
                        p.wait(VEC, ht)
                        if bi >= 2:
                            p.wait(VEC, band_store_tok[bi - 2])
                        v = VEC.tensor_sub(spl[s][:, 0:dout], stg[s][:, 0:dout], sph[s][:, 0:dout])
                        st = p.inc(v, "ve")
                        band_split_tok[bi] = st
                        p.wait(GPS, st)
                        p.dma(GPS, "lg", dsth, sph[s][:, 0:dout])
                        band_store_tok[bi] = p.dma(GPS, "lg", dstl, spl[s][:, 0:dout])

                    p.full_barrier()

                with nc.sbuf_tensor("kraw", [128, 2 * 512], F32) as kraw, \
                     nc.sbuf_tensor("v_sb", [128, 2 * 512], F32) as v_sb, \
                     nc.sbuf_tensor("mb_sb", [128, 2 * D], F32) as mb_sb, \
                     nc.sbuf_tensor("a_sb", [128, 1024], F32) as a_sb, \
                     nc.sbuf_tensor("mk_sb", [128, HKV * 768], F32) as mk_sb, \
                     nc.sbuf_tensor("mkT_sb", [128, HKV * 768], F32) as mkT_sb, \
                     nc.sbuf_tensor("rhsg", [128, H * 129], F32) as rhsg, \
                     nc.sbuf_tensor("dtmp", [128, 258], F32) as dtmp, \
                     nc.sbuf_tensor("ttmp", [128, 256], F32) as ttmp:

                    dtmp_tok = [None, None]
                    cend_pe = {}
                    cdpfp = {}
                    xT_read = {}
                    xtm_read = None
                    proj_done = {}
                    proj_mm = {}
                    kraw_read = {}

                    def p1_T(g):
                        nonlocal xtm_read
                        sl = g % 2
                        if xtm_read is not None:
                            p.wait(SYN, xtm_read)
                        t = SYN.dma_start(out=x_tm[:], in_=x_d.ap()[g * 128:(g + 1) * 128, :])
                        xt = p.inc(t, "lw", 16)
                        srcs = [x_tm[:, kc * 128:(kc + 1) * 128] for kc in range(KC)]
                        dsth = [xT_h[:, (sl * KC + kc) * 128:(sl * KC + kc + 1) * 128] for kc in range(KC)]
                        dstl = [xT_l[:, (sl * KC + kc) * 128:(sl * KC + kc + 1) * 128] for kc in range(KC)]
                        et = transpose_batch(srcs, dsth, dstl, xt, extra_wait=xT_read.get(sl))
                        xtm_read = p.tok("pe")
                        return et

                    def p1_P(g, xs_tok):
                        sl = g % 2
                        slices = []
                        for ds in range(2):
                            slices.append((cache["wk"][0], cache["wk"][1], ds * 256, BOFF["k"] + ds * 256,
                                           (lambda ti, ds=ds: kraw[:, ti * 512 + ds * 256: ti * 512 + ds * 256 + 256]), None))
                        for ds in range(2):
                            slices.append((cache["wv"][0], cache["wv"][1], ds * 256, BOFF["v"] + ds * 256,
                                           (lambda ti, ds=ds: v_sb[:, ti * 512 + ds * 256: ti * 512 + ds * 256 + 256]), None))
                        for ds in range(8):
                            slices.append((cache["wmb"][0], cache["wmb"][1], ds * 256, BOFF["mb"] + ds * 256,
                                           (lambda ti, ds=ds: mb_sb[:, ti * D + ds * 256: ti * D + ds * 256 + 256]),
                                           AF.Sigmoid))
                        kt = kraw_read.get(g - 2)
                        if kt is not None:
                            for t_ in kt:
                                p.wait(VEC, t_)
                                p.wait(ACT, t_)
                        mt = proj_pairs(slices, xs_tok,
                                        (lambda ti, kc: xT_h[:, (ti * KC + kc) * 128:(ti * KC + kc + 1) * 128]),
                                        (lambda ti, kc: xT_l[:, (ti * KC + kc) * 128:(ti * KC + kc + 1) * 128]),
                                        prev_used=proj_mm.get(g - 1), tis=[sl])
                        proj_mm[g] = mt
                        xT_read[sl] = mt
                        proj_done[g] = [p.tok("ve"), p.tok("se")]

                    def p1_C(g):
                        ti = g % 2
                        p.wait_all(ACT, proj_done[g])
                        p.wait_all(VEC, proj_done[g])
                        mkt = dpfp(a_sb[:], mk_sb[:], HKV, kraw[:, ti * 512:(ti + 1) * 512],
                                   pe_wait=cend_pe.get(g - 1), ve_wait=cdpfp.get(g - 1))
                        cdpfp[g] = mkt
                        nt = mk_normalize(mk_sb[:], HKV, C_NSQA)
                        srcs = []
                        dsth = []
                        for j in range(HKV):
                            for fc in range(FCN):
                                srcs.append(mk_sb[:, j * 768 + fc * 128: j * 768 + (fc + 1) * 128])
                                dsth.append(mkT_sb[:, (j * FCN + fc) * 128:(j * FCN + fc + 1) * 128])
                        mkT_tok = transpose_batch(srcs, dsth, None, nt)

                        for h in range(H):
                            kv = h // 4
                            bank, bn = (psE1, "psE1") if h % 2 == 0 else (psE2, "psE2")
                            p.acquire(TEN, bn)
                            p.wait(TEN, mkT_tok)
                            p.wait(TEN, tok_memn)
                            last = None
                            for fc in range(FCN):
                                last = TEN.matmul(bank[:, 0:129],
                                                  mkT_sb[:, (kv * FCN + fc) * 128:(kv * FCN + fc + 1) * 128],
                                                  memn[:, (h * FCN + fc) * 129:(h * FCN + fc + 1) * 129],
                                                  start=(fc == 0), stop=(fc == FCN - 1))
                            mt = p.inc(last, "pe")
                            p.wait(VEC, mt)
                            VEC.tensor_scalar(small[:, C_DENP + h:C_DENP + h + 1],
                                              bank[:, 128:129],
                                              small[:, C_RINV + kv:C_RINV + kv + 1], EPS_A,
                                              op0=ALU.mult, op1=ALU.add)
                            VEC.drain()
                            VEC.reciprocal(small[:, C_RD + h:C_RD + h + 1],
                                           small[:, C_DENP + h:C_DENP + h + 1])
                            VEC.drain()
                            v = VEC.tensor_mul(small[:, C_RDP + h:C_RDP + h + 1],
                                               small[:, C_RD + h:C_RD + h + 1],
                                               small[:, C_RINV + kv:C_RINV + kv + 1])
                            vt = p.inc(v, "ve")
                            p.wait(ACT, vt)
                            s = ACT.activation(out=ttmp[:, 0:128], in_=bank[:, 0:128],
                                               func=AF.Identity, scale=small[:, C_RDP + h:C_RDP + h + 1])
                            st = p.inc(s, "se")
                            p.release(bn, st)
                            p.wait(VEC, st)
                            VEC.tensor_sub(ttmp[:, 128:256], v_sb[:, ti * 512 + kv * 128: ti * 512 + (kv + 1) * 128],
                                           ttmp[:, 0:128])
                            VEC.drain()
                            v = VEC.scalar_tensor_tensor(
                                out=rhsg[:, h * 129:h * 129 + 128], in0=ttmp[:, 128:256],
                                scalar=small[:, C_RINV + kv:C_RINV + kv + 1],
                                in1=mb_sb[:, ti * D + h * 128: ti * D + (h + 1) * 128],
                                op0=ALU.mult, op1=ALU.mult)
                            p.inc(v, "ve")

                        kraw_read[g] = [p.tok("se"), p.tok("ve")]
                        for kv in range(HKV):
                            VEC.tensor_scalar_mul(small[:, C_COEF + kv * 4:C_COEF + (kv + 1) * 4],
                                                  small[:, C_DENP + kv * 4:C_DENP + (kv + 1) * 4],
                                                  small[:, C_RMKSQ + kv:C_RMKSQ + kv + 1])
                        VEC.drain()
                        VEC.tensor_scalar(small[:, C_COEF:C_COEF + H], small[:, C_COEF:C_COEF + H],
                                          -1.0, 1.0, op0=ALU.mult, op1=ALU.add)
                        VEC.drain()
                        VEC.tensor_scalar_max(small[:, C_COEF:C_COEF + H], small[:, C_COEF:C_COEF + H], 0.0)
                        VEC.drain()
                        VEC.tensor_scalar_min(small[:, C_COEF:C_COEF + H], small[:, C_COEF:C_COEF + H], 1.0)
                        VEC.drain()
                        for kv in range(HKV):
                            VEC.tensor_scalar_mul(small[:, C_COEF + kv * 4:C_COEF + (kv + 1) * 4],
                                                  small[:, C_COEF + kv * 4:C_COEF + (kv + 1) * 4],
                                                  small[:, C_RINV + kv:C_RINV + kv + 1])
                        VEC.drain()
                        v = VEC.tensor_copy(rhsg.ap().rearrange("p (h w) -> p h w", w=129)[:, :, 128],
                                            small[:, C_COEF:C_COEF + H])
                        rhs_tok = p.inc(v, "ve")

                        for h in range(H):
                            kv = h // 4
                            for fc in range(FCN):
                                i = h * FCN + fc
                                bank, bn = (psE1, "psE1") if i % 2 == 0 else (psE2, "psE2")
                                p.acquire(TEN, bn)
                                p.wait(TEN, rhs_tok)
                                last = TEN.matmul(bank[:, 0:129],
                                                  mk_sb[:, kv * 768 + fc * 128: kv * 768 + (fc + 1) * 128],
                                                  rhsg[:, h * 129:(h + 1) * 129],
                                                  start=True, stop=True)
                                mt = p.inc(last, "pe")
                                slx = i % 2
                                if dtmp_tok[slx] is not None:
                                    p.wait(VEC, dtmp_tok[slx])
                                p.wait(VEC, mt)
                                ev = VEC.tensor_copy(dtmp[:, slx * 129:(slx + 1) * 129], bank[:, 0:129])
                                et = p.inc(ev, "ve")
                                p.release(bn, et)
                                p.wait(GPS, et)
                                p.wait(GPS, tok_dz)
                                d = GPS.dma_start(out=delta_d.ap()[:, i * 129:(i + 1) * 129],
                                                  in_=dtmp[:, slx * 129:(slx + 1) * 129],
                                                  accum_op=ALU.add)
                                dtmp_tok[slx] = p.inc(d, "lg", 16)
                        cend_pe[g] = p.tok("pe")

                    for g in range(NTC):
                        xs = p1_T(g)
                        p1_P(g, xs)
                        if g > 0:
                            p1_C(g - 1)
                    p1_C(NTC - 1)

                    p.full_barrier()
                    c = GPS.collective_compute(
                        "AllReduce", ALU.add,
                        replica_groups=[list(range(NCORES))],
                        ins=[delta_d.ap().opt()],
                        outs=[ccout_d.ap().opt()],
                    )
                    ct = p.inc(c, "cc")
                    p.wait(SYN, ct)
                    CH = 3072
                    for i in range((MEMN_W + CH - 1) // CH):
                        w = min(CH, MEMN_W - i * CH)
                        t = SYN.dma_start(out=mk_sb[:, 0:w] if i % 2 == 0 else mkT_sb[:, 0:w],
                                          in_=ccout_d.ap()[:, i * CH:i * CH + w])
                        lt = p.inc(t, "lw", 16)
                        p.wait(VEC, lt)
                        buf = mk_sb if i % 2 == 0 else mkT_sb
                        v = VEC.tensor_add(memn[:, i * CH:i * CH + w], memn[:, i * CH:i * CH + w],
                                           buf[:, 0:w])
                        vt = p.inc(v, "ve")
                        p.wait(SYN, vt)
                    p.full_barrier()

                with nc.sbuf_tensor("q_sb", [128, 2 * D], F32) as q_sb, \
                     nc.sbuf_tensor("gate_sb", [128, 2 * D], F32) as gate_sb, \
                     nc.sbuf_tensor("aq_sb", [128, 512], F32) as aq_sb, \
                     nc.sbuf_tensor("mq_sb", [128, 2 * 768], F32) as mq_sb, \
                     nc.sbuf_tensor("mqT_sb", [128, 2 * 768], F32) as mqT_sb, \
                     nc.sbuf_tensor("y1T_th", [128, D], BF16) as y1T_th, \
                     nc.sbuf_tensor("y1T_tl", [128, D], BF16) as y1T_tl, \
                     nc.sbuf_tensor("ttmp3", [128, 256], F32) as ttmp:

                    xT_read = {}
                    xtm_read = None
                    xtm_free = None
                    proj_done = {}
                    proj_mm = {}
                    qread = {}
                    cend_pe = {}
                    cdpfp = {}
                    y1T_used = None
                    y1T_store_tok = None

                    def p3_T(g):
                        nonlocal xtm_read
                        sl = g % 2
                        if xtm_read is not None:
                            p.wait(SYN, xtm_read)
                        if xtm_free is not None:
                            for t_ in xtm_free:
                                p.wait(SYN, t_)
                        t = SYN.dma_start(out=x_tm[:], in_=x_d.ap()[g * 128:(g + 1) * 128, :])
                        xt = p.inc(t, "lw", 16)
                        srcs = [x_tm[:, kc * 128:(kc + 1) * 128] for kc in range(KC)]
                        dsth = [xT_h[:, (sl * KC + kc) * 128:(sl * KC + kc + 1) * 128] for kc in range(KC)]
                        dstl = [xT_l[:, (sl * KC + kc) * 128:(sl * KC + kc + 1) * 128] for kc in range(KC)]
                        et = transpose_batch(srcs, dsth, dstl, xt, extra_wait=xT_read.get(sl))
                        xtm_read = p.tok("pe")
                        return et

                    def p3_P(g, xs_tok):
                        sl = g % 2
                        slices = []
                        for ds in range(8):
                            slices.append((cache["wq"][0], cache["wq"][1], ds * 256, BOFF["q"] + ds * 256,
                                           (lambda ti, ds=ds: q_sb[:, ti * D + ds * 256: ti * D + ds * 256 + 256]), None))
                        for ds in range(8):
                            slices.append((cache["wg"][0], cache["wg"][1], ds * 256, BOFF["g"] + ds * 256,
                                           (lambda ti, ds=ds: gate_sb[:, ti * D + ds * 256: ti * D + ds * 256 + 256]),
                                           AF.Sigmoid))
                        qt = qread.get(g - 2)
                        if qt is not None:
                            for t_ in qt:
                                p.wait(VEC, t_)
                                p.wait(ACT, t_)
                        mt = proj_pairs(slices, xs_tok,
                                        (lambda ti, kc: xT_h[:, (ti * KC + kc) * 128:(ti * KC + kc + 1) * 128]),
                                        (lambda ti, kc: xT_l[:, (ti * KC + kc) * 128:(ti * KC + kc + 1) * 128]),
                                        prev_used=proj_mm.get(g - 1), tis=[sl])
                        proj_mm[g] = mt
                        xT_read[sl] = mt
                        proj_done[g] = [p.tok("ve"), p.tok("se")]

                    def p3_C(g):
                        nonlocal xtm_free, y1T_used, y1T_store_tok
                        ti = g % 2
                        tc = g
                        p.wait(SYN, xtm_read)
                        if xtm_free is not None:
                            for t_ in xtm_free:
                                p.wait(SYN, t_)
                        p.wait(SYN, cdpfp.get(g - 1))
                        p.wait(SYN, qread.get(g - 1) and qread[g - 1][1])
                        t = SYN.dma_start(out=x_tm[:], in_=x_d.ap()[tc * 128:(tc + 1) * 128, :])
                        x_tok = p.inc(t, "lw", 16)

                        p.wait_all(ACT, proj_done[g])
                        p.wait_all(VEC, proj_done[g])
                        for hg in range(8):
                            h0 = hg * 2
                            mkt = dpfp(aq_sb[:], mq_sb[:], 2,
                                       q_sb[:, ti * D + h0 * 128: ti * D + (h0 + 2) * 128],
                                       pe_wait=(cend_pe.get(g - 1) if hg == 0 else None),
                                       ve_wait=(cdpfp.get(g - 1) if hg == 0 else None))
                            cdpfp[g] = mkt
                            nt = mk_normalize(mq_sb[:], 2, C_NSQA)
                            srcs = []
                            dsth = []
                            for j in range(2):
                                for fc in range(FCN):
                                    srcs.append(mq_sb[:, j * 768 + fc * 128: j * 768 + (fc + 1) * 128])
                                    dsth.append(mqT_sb[:, (j * FCN + fc) * 128:(j * FCN + fc + 1) * 128])
                            mqT_tok = transpose_batch(srcs, dsth, None, mkt)

                            for j in range(2):
                                h = h0 + j
                                bank, bn = (psE1, "psE1") if j % 2 == 0 else (psE2, "psE2")
                                p.acquire(TEN, bn)
                                p.wait(TEN, mqT_tok)
                                last = None
                                for fc in range(FCN):
                                    last = TEN.matmul(bank[:, 0:129],
                                                      mqT_sb[:, (j * FCN + fc) * 128:(j * FCN + fc + 1) * 128],
                                                      memn[:, (h * FCN + fc) * 129:(h * FCN + fc + 1) * 129],
                                                      start=(fc == 0), stop=(fc == FCN - 1))
                                mt = p.inc(last, "pe")
                                p.wait(VEC, mt)
                                VEC.tensor_scalar(small[:, C_DENP + h:C_DENP + h + 1],
                                                  bank[:, 128:129],
                                                  small[:, C_RINV + j:C_RINV + j + 1], EPS_A,
                                                  op0=ALU.mult, op1=ALU.add)
                                VEC.drain()
                                VEC.reciprocal(small[:, C_RD + h:C_RD + h + 1],
                                               small[:, C_DENP + h:C_DENP + h + 1])
                                VEC.drain()
                                v = VEC.tensor_mul(small[:, C_RDP + h:C_RDP + h + 1],
                                                   small[:, C_RD + h:C_RD + h + 1],
                                                   small[:, C_RINV + j:C_RINV + j + 1])
                                vt = p.inc(v, "ve")
                                p.wait(ACT, vt)
                                s = ACT.activation(out=ttmp[:, 0:128], in_=bank[:, 0:128],
                                                   func=AF.Identity, scale=small[:, C_RDP + h:C_RDP + h + 1])
                                st = p.inc(s, "se")
                                p.release(bn, st)
                                p.wait(VEC, st)
                                p.wait(VEC, x_tok)
                                VEC.tensor_mul(ttmp[:, 128:256], ttmp[:, 0:128],
                                               gate_sb[:, ti * D + h * 128: ti * D + (h + 1) * 128])
                                VEC.drain()
                                v = VEC.scalar_tensor_tensor(
                                    out=x_tm[:, h * 128:(h + 1) * 128],
                                    in0=x_tm[:, h * 128:(h + 1) * 128], scalar=2.0,
                                    in1=ttmp[:, 128:256], op0=ALU.mult, op1=ALU.add)
                                p.inc(v, "ve")
                        qread[g] = [p.tok("se"), p.tok("ve")]

                        lt = ln_chain(x_tm[:], x_tm[:])
                        p.wait(GPS, lt)
                        p.dma(GPS, "lg", y1_d.ap()[tc * 128:(tc + 1) * 128, :], x_tm[:])
                        if y1T_used is not None:
                            p.wait(VEC, y1T_used)
                        if y1T_store_tok is not None:
                            p.wait(VEC, y1T_store_tok)
                        srcs = [x_tm[:, kc * 128:(kc + 1) * 128] for kc in range(KC)]
                        dsth = [y1T_th[:, kc * 128:(kc + 1) * 128] for kc in range(KC)]
                        dstl = [y1T_tl[:, kc * 128:(kc + 1) * 128] for kc in range(KC)]
                        yt = transpose_batch(srcs, dsth, dstl, lt)
                        p.wait(GPS, yt)
                        p.dma(GPS, "lg", y1T_h_d.ap()[tc], y1T_th[:])
                        y1T_st = p.dma(GPS, "lg", y1T_l_d.ap()[tc], y1T_tl[:])
                        xtm_free = [p.tok("pe"), p.tok("lg")]
                        y1T_store_tok = y1T_st

                        p.acquire(TEN, "psG")
                        p.wait(TEN, yt)
                        last = None
                        for kc in range(KC):
                            yh = y1T_th[:, kc * 128:(kc + 1) * 128]
                            yl = y1T_tl[:, kc * 128:(kc + 1) * 128]
                            gh = gwT_h[:, kc * 8:(kc + 1) * 8]
                            gl = gwT_l[:, kc * 8:(kc + 1) * 8]
                            last = TEN.matmul(psG[0:8, 0:128], gh, yh, start=(kc == 0), stop=False)
                            last = TEN.matmul(psG[0:8, 0:128], gl, yh, start=False, stop=False)
                            last = TEN.matmul(psG[0:8, 0:128], gh, yl, start=False,
                                              stop=(kc == KC - 1))
                        mt = p.inc(last, "pe")
                        p.wait(VEC, mt)
                        v = VEC.tensor_copy(small[0:8, C_LGT:C_LGT + 128], psG[0:8, 0:128])
                        vt = p.inc(v, "ve")
                        p.wait(TEN, vt)
                        tr = TEN.transpose(psG[:, 128:136], small[0:8, C_LGT:C_LGT + 128],
                                           ident[0:8, 0:8])
                        mt = p.inc(tr, "pe")
                        p.wait(VEC, mt)
                        lg = small[:, C_LTM:C_LTM + 8]
                        v = VEC.tensor_copy(lg, psG[:, 128:136])
                        p.release("psG", p.inc(v, "ve"))
                        VEC.drain()
                        g8 = lambda i: small[:, C_G8 + i * 8:C_G8 + (i + 1) * 8]
                        max8, lz1, lz2, msk1, msk2, mr1, mr2 = (g8(i) for i in range(7))
                        VEC.max(out=max8, in_=lg)
                        VEC.memset(mr1, -1e30)
                        VEC.memset(mr2, -1e30)
                        VEC.drain()
                        VEC.tensor_copy(mr1[:, 0:1], max8[:, 0:1])
                        VEC.tensor_copy(mr2[:, 0:1], max8[:, 1:2])
                        VEC.drain()
                        VEC.match_replace(out=lz1, in_to_replace=mr1, in_values=lg, imm_value=-1e38)
                        VEC.drain()
                        VEC.match_replace(out=lz2, in_to_replace=mr2, in_values=lz1, imm_value=-1e38)
                        VEC.drain()
                        VEC.tensor_tensor(msk1, lz1, lg, op=ALU.not_equal)
                        VEC.tensor_tensor(msk2, lz2, lz1, op=ALU.not_equal)
                        v = VEC.tensor_sub(small[:, C_GS:C_GS + 1], max8[:, 0:1], max8[:, 1:2])
                        vt = p.inc(v, "ve")
                        p.wait(ACT, vt)
                        ACT.activation(out=small[:, C_GS + 1:C_GS + 2], in_=small[:, C_GS:C_GS + 1],
                                       func=AF.Sigmoid)
                        s = ACT.activation(out=small[:, C_GS + 2:C_GS + 3], in_=small[:, C_GS:C_GS + 1],
                                           func=AF.Sigmoid, scale=-1.0)
                        st = p.inc(s, "se")
                        p.wait(VEC, st)
                        VEC.tensor_scalar_mul(msk1, msk1, small[:, C_GS + 1:C_GS + 2])
                        VEC.tensor_scalar_mul(msk2, msk2, small[:, C_GS + 2:C_GS + 3])
                        VEC.drain()
                        v = VEC.tensor_add(cmb_all[:, tc * 8:(tc + 1) * 8], msk1, msk2)
                        p.inc(v, "ve")
                        y1T_used = p.tok("pe")
                        cend_pe[g] = p.tok("pe")

                    for g in range(NTC):
                        xs = p3_T(g)
                        p3_P(g, xs)
                        if g > 0:
                            p3_C(g - 1)
                    p3_C(NTC - 1)

                    p.full_barrier()

            FG = 2
            with nc.sbuf_tensor("y1Tg_h", [128, FG * D], BF16) as y1Tg_h, \
                 nc.sbuf_tensor("y1Tg_l", [128, FG * D], BF16) as y1Tg_l, \
                 nc.sbuf_tensor("sg_g", [128, FG * 512], F32) as sg_g, \
                 nc.sbuf_tensor("act_g", [128, FG * II], F32) as act_g, \
                 nc.sbuf_tensor("actT_h", [128, FG * II], BF16) as actT_h, \
                 nc.sbuf_tensor("actT_l", [128, FG * II], BF16) as actT_l, \
                 nc.sbuf_tensor("out_acc", [128, FG * D], F32) as out_acc, \
                 nc.sbuf_tensor("wt3_h", [128, 8192], BF16) as wt3_h, \
                 nc.sbuf_tensor("wt3_l", [128, 8192], BF16) as wt3_l, \
                 nc.sbuf_tensor("wt4_h", [128, 8192], BF16) as wt4_h, \
                 nc.sbuf_tensor("wt4_l", [128, 8192], BF16) as wt4_l, \
                 nc.sbuf_tensor("y1ld", [128, D], F32) as y1ld:

                outb = act_g
                wsets = [((wt1_h, wt1_l), (wt2_h, wt2_l)),
                         ((wt3_h, wt3_l), (wt4_h, wt4_l))]
                final_store = None
                y1g_used = None
                pending_final = None

                def emit_final(gf, otoks):
                    nonlocal final_store
                    for ti in range(FG):
                        tc = gf * FG + ti
                        if final_store is not None:
                            p.wait(GPS, final_store)
                        t = GPS.dma_start(out=y1ld[:], in_=y1_d.ap()[tc * 128:(tc + 1) * 128, :])
                        lt = p.inc(t, "lg", 16)
                        p.wait(VEC, lt)
                        for dsx in range(4):
                            p.wait(VEC, otoks.get((ti, dsx)))
                        v = VEC.tensor_add(y1ld[:], y1ld[:], out_acc[:, ti * D:(ti + 1) * D])
                        vt = p.inc(v, "ve")
                        if final_store is not None:
                            p.wait(ACT, final_store)
                        lnt = ln_chain(y1ld[:], y1ld[:])
                        p.wait(GPS, lnt)
                        t = GPS.dma_start(out=out_d.ap()[tc * 128:(tc + 1) * 128, :], in_=y1ld[:])
                        final_store = p.inc(t, "lg", 16)

                for g in range(NTC // FG):
                    if y1g_used is not None:
                        p.wait(SYN, y1g_used)
                    p.dma(SYN, "lw", y1Tg_h.ap().rearrange("p (t c) -> p t c", t=FG),
                          y1T_h_d.ap()[g * FG:(g + 1) * FG].rearrange("t p c -> p t c"))
                    y1g_tok = p.dma(SYN, "lw", y1Tg_l.ap().rearrange("p (t c) -> p t c", t=FG),
                                    y1T_l_d.ap()[g * FG:(g + 1) * FG].rearrange("t p c -> p t c"))

                    stages = []
                    for e in range(E):
                        for (c0, w) in W1_SLICES:
                            stages.append(("w1", e, c0, w))
                        for dsp in range(2):
                            stages.append(("w2", e, dsp))
                    load_toks = {}
                    mm_toks = {}

                    def emit_ffn_load(si):
                        kind, e, a1, a2 = stages[si][0], stages[si][1], stages[si][2], stages[si][3] if len(stages[si]) > 3 else None
                        (b1h, b1l), (b2h, b2l) = wsets[si % 2]
                        if si >= 2 and (si - 2) in mm_toks:
                            p.wait(SYN, mm_toks[si - 2])
                        if kind == "w1":
                            c0, w = a1, a2
                            last = None
                            for kc in range(KC):
                                p.dma(SYN, "lw", b1h[:, kc * 512:kc * 512 + w],
                                      cache["w1"][0].ap()[e, kc * 128:(kc + 1) * 128, c0:c0 + w])
                                p.dma(SYN, "lw", b1l[:, kc * 512:kc * 512 + w],
                                      cache["w1"][1].ap()[e, kc * 128:(kc + 1) * 128, c0:c0 + w])
                                p.dma(SYN, "lw", b2h[:, kc * 512:kc * 512 + w],
                                      cache["w1"][0].ap()[e, kc * 128:(kc + 1) * 128, 1408 + c0:1408 + c0 + w])
                                last = p.dma(SYN, "lw", b2l[:, kc * 512:kc * 512 + w],
                                             cache["w1"][1].ap()[e, kc * 128:(kc + 1) * 128, 1408 + c0:1408 + c0 + w])
                        else:
                            dsp = a1
                            ds0, ds1 = 2 * dsp, 2 * dsp + 1
                            last = None
                            for ic in range(ICN):
                                p.dma(SYN, "lw", b1h[:, ic * 512:(ic + 1) * 512],
                                      cache["w2"][0].ap()[e, ic * 128:(ic + 1) * 128, ds0 * 512:(ds0 + 1) * 512])
                                p.dma(SYN, "lw", b1l[:, ic * 512:(ic + 1) * 512],
                                      cache["w2"][1].ap()[e, ic * 128:(ic + 1) * 128, ds0 * 512:(ds0 + 1) * 512])
                                p.dma(SYN, "lw", b2h[:, ic * 512:(ic + 1) * 512],
                                      cache["w2"][0].ap()[e, ic * 128:(ic + 1) * 128, ds1 * 512:(ds1 + 1) * 512])
                                last = p.dma(SYN, "lw", b2l[:, ic * 512:(ic + 1) * 512],
                                             cache["w2"][1].ap()[e, ic * 128:(ic + 1) * 128, ds1 * 512:(ds1 + 1) * 512])
                        load_toks[si] = last

                    oacc_toks = {}
                    act_tok = None
                    aT_tok = None
                    w2mm_tok = None
                    sg_read = {}
                    emit_ffn_load(0)
                    for si, st_ in enumerate(stages):
                        if si + 1 < len(stages):
                            emit_ffn_load(si + 1)
                        kind, e = st_[0], st_[1]
                        (b1h, b1l), (b2h, b2l) = wsets[si % 2]
                        lt = load_toks[si]
                        if kind == "w1":
                            c0, w = st_[2], st_[3]
                            for ti in range(FG):
                                tc = g * FG + ti
                                bks = [(psA, "psA"), (psB, "psB")] if ti % 2 == 0 else [(psE1, "psE1"), (psE2, "psE2")]
                                mt = mm3m(
                                    (lambda kc, ti=ti: y1Tg_h[:, (ti * KC + kc) * 128:(ti * KC + kc + 1) * 128]),
                                    (lambda kc, ti=ti: y1Tg_l[:, (ti * KC + kc) * 128:(ti * KC + kc + 1) * 128]),
                                    [(bks[0][0], bks[0][1],
                                      (lambda kc: b1h[:, kc * 512:kc * 512 + w]),
                                      (lambda kc: b1l[:, kc * 512:kc * 512 + w]), w, None),
                                     (bks[1][0], bks[1][1],
                                      (lambda kc: b2h[:, kc * 512:kc * 512 + w]),
                                      (lambda kc: b2l[:, kc * 512:kc * 512 + w]), w, None)],
                                    KC, waits=[y1g_tok, lt])
                                mm_toks[si] = mt
                                p.wait(ACT, mt)
                                if sg_read.get(ti) is not None:
                                    p.wait(ACT, sg_read[ti])
                                s = ACT.activation(out=sg_g[:, ti * 512:ti * 512 + w],
                                                   in_=bks[0][0][:, 0:w], func=AF.Silu)
                                stk = p.inc(s, "se")
                                p.release(bks[0][1], stk)
                                p.wait(VEC, stk)
                                v = VEC.scalar_tensor_tensor(
                                    out=act_g[:, ti * II + c0: ti * II + c0 + w],
                                    in0=bks[1][0][:, 0:w], scalar=cmb_all[:, tc * 8 + e:tc * 8 + e + 1],
                                    in1=sg_g[:, ti * 512:ti * 512 + w],
                                    op0=ALU.mult, op1=ALU.mult)
                                vt = p.inc(v, "ve")
                                p.release(bks[1][1], vt)
                                sg_read[ti] = vt
                                act_tok = vt
                            if st_[2] == W1_SLICES[-1][0]:
                                for ti in range(FG):
                                    if w2mm_tok is not None:
                                        p.wait(VEC, w2mm_tok)
                                    srcs = [act_g[:, ti * II + ic * 128: ti * II + (ic + 1) * 128] for ic in range(ICN)]
                                    dsth = [actT_h[:, (ti * ICN + ic) * 128:(ti * ICN + ic + 1) * 128] for ic in range(ICN)]
                                    dstl = [actT_l[:, (ti * ICN + ic) * 128:(ti * ICN + ic + 1) * 128] for ic in range(ICN)]
                                    aT_tok = transpose_batch(srcs, dsth, dstl, act_tok)
                        else:
                            dsp = st_[2]
                            ds0, ds1 = 2 * dsp, 2 * dsp + 1
                            for ti in range(FG):
                                bks = [(psA, "psA"), (psB, "psB")] if ti % 2 == 0 else [(psE1, "psE1"), (psE2, "psE2")]
                                mt = mm3m(
                                    (lambda ic, ti=ti: actT_h[:, (ti * ICN + ic) * 128:(ti * ICN + ic + 1) * 128]),
                                    (lambda ic, ti=ti: actT_l[:, (ti * ICN + ic) * 128:(ti * ICN + ic + 1) * 128]),
                                    [(bks[0][0], bks[0][1],
                                      (lambda ic: b1h[:, ic * 512:(ic + 1) * 512]),
                                      (lambda ic: b1l[:, ic * 512:(ic + 1) * 512]), 512, None),
                                     (bks[1][0], bks[1][1],
                                      (lambda ic: b2h[:, ic * 512:(ic + 1) * 512]),
                                      (lambda ic: b2l[:, ic * 512:(ic + 1) * 512]), 512, None)],
                                    ICN, waits=[aT_tok, lt])
                                mm_toks[si] = mt
                                w2mm_tok = mt
                                for i, ds in enumerate((ds0, ds1)):
                                    bank, bn = bks[i]
                                    p.wait(VEC, mt)
                                    dsl = out_acc[:, ti * D + ds * 512: ti * D + (ds + 1) * 512]
                                    if e == 0:
                                        ev = VEC.tensor_copy(dsl, bank[:, 0:512])
                                    else:
                                        k = oacc_toks.get((ti, ds))
                                        if k is not None:
                                            p.wait(VEC, k)
                                        ev = VEC.tensor_add(dsl, dsl, bank[:, 0:512])
                                    et = p.inc(ev, "ve")
                                    oacc_toks[(ti, ds)] = et
                                    p.release(bn, et)

                    emit_final(g, dict(oacc_toks))
                    y1g_used = p.tok("pe")

                p.wait(GPS, final_store)

    es_sem.close()
    es_ps.close()
    es_perm.close()
    return nc


_NC_CACHE = None


def kernel(**inputs):
    global _NC_CACHE
    x = np.asarray(inputs["L0_attn_hidden_states"], dtype=np.float32).reshape(TOK * NCORES, D)
    Wq = np.asarray(inputs["L0_attn_Wq"], dtype=np.float32)
    bq = np.asarray(inputs["L0_attn_bq"], dtype=np.float32)
    Wk = np.asarray(inputs["L0_attn_Wk"], dtype=np.float32)
    bk = np.asarray(inputs["L0_attn_bk"], dtype=np.float32)
    Wv = np.asarray(inputs["L0_attn_Wv"], dtype=np.float32)
    bv = np.asarray(inputs["L0_attn_bv"], dtype=np.float32)
    Wg = np.asarray(inputs["L0_attn_Wg"], dtype=np.float32)
    bg = np.asarray(inputs["L0_attn_bg"], dtype=np.float32)
    Wmb = np.asarray(inputs["L0_attn_Wmb"], dtype=np.float32)
    bmb = np.asarray(inputs["L0_attn_bmb"], dtype=np.float32)
    memory = np.asarray(inputs["L0_attn_memory"], dtype=np.float32)
    norm = np.asarray(inputs["L0_attn_norm"], dtype=np.float32)
    gate_w = np.asarray(inputs["L0_ffn_gate_w"], dtype=np.float32)
    w1 = np.asarray(inputs["L0_ffn_w1"], dtype=np.float32)
    w2 = np.asarray(inputs["L0_ffn_w2"], dtype=np.float32)

    WqT = np.ascontiguousarray(Wq.T)
    WkT = np.ascontiguousarray(Wk.T)
    WvT = np.ascontiguousarray(Wv.T)
    WgT = np.ascontiguousarray(Wg.T)
    WmbT = np.ascontiguousarray(Wmb.T)
    gwT = np.ascontiguousarray(gate_w.T)
    w1T = np.ascontiguousarray(np.transpose(w1, (0, 2, 1)))
    w2T = np.ascontiguousarray(np.transpose(w2, (0, 2, 1)))
    mem_r = memory.reshape(H, FCN, 128, DH).transpose(2, 0, 1, 3)
    nrm_r = norm.reshape(H, FCN, 128).transpose(2, 0, 1)[..., None]
    memn = np.ascontiguousarray(
        np.concatenate([mem_r, nrm_r], axis=-1).reshape(128, MEMN_W).astype(np.float32))
    bpack = np.concatenate([bq, bk, bv, bg, bmb]).reshape(1, 7168).astype(np.float32)
    ident = np.eye(128, dtype=np.float32)

    if _NC_CACHE is None:
        _NC_CACHE = build_core()
    nc = _NC_CACHE

    shared = {"WqT": WqT, "WkT": WkT, "WvT": WvT, "WgT": WgT, "WmbT": WmbT,
              "gwT": gwT, "w1T": w1T, "w2T": w2T, "memn": memn,
              "bpack": bpack, "ident": ident}
    in_maps = [dict(shared, x=np.ascontiguousarray(x[c * TOK:(c + 1) * TOK]))
               for c in range(NCORES)]
    res = run_bass_kernel_spmd(nc, in_maps, core_ids=list(range(NCORES)))
    out = np.concatenate([r["out"] for r in res.results], axis=0)
    return out.reshape(2, 8192, 2048).astype(np.float32)


if __name__ == "__main__":
    pass
